# version 19
# baseline (speedup 1.0000x reference)
import numpy as np
import ml_dtypes
import concourse.bass as bass
import concourse.mybir as mybir
from concourse.bass_utils import run_bass_kernel_spmd
from contextlib import ExitStack

F32 = mybir.dt.float32
BF16 = mybir.dt.bfloat16
U8 = mybir.dt.uint8
AF = mybir.ActivationFunctionType
ALU = mybir.AluOpType
AX = mybir.AxisListType.X
BF = ml_dtypes.bfloat16

D = 4096
DC = 32
TOK = 1024
HD = 128
KR = 1024
S_ATTN = float(1.0 / np.sqrt(128.0))
EPS = 1e-5
INVD = float(1.0 / 4096.0)

DEV_STOP = None


class Pl:
    ENGS = ("tensor", "scalar", "vector", "gpsimd", "sync")

    NL = 4

    def __init__(self):
        self.ops = {e: [] for e in self.ENGS}
        self.pe = 0
        self.act = 0
        self.dve = 0
        self.gpn = 0
        self.cc = 0
        self.sems = {}
        self.rr = {"gpsimd": 0, "sync": 0}
        self.lane_cnt = {}
        for b in ("dg", "ds"):
            for l in range(self.NL):
                self.lane_cnt[b + str(l)] = 0

    def raw(self, eng, fn):
        self.ops[eng].append(fn)

    def wait(self, eng, sem, val):
        if val <= 0:
            return
        self.raw(eng, lambda E, s=sem, v=val: E.wait_ge(self.sems[s], v))

    def mm(self, out, lhsT, rhs, start=True, stop=True, inc=False):
        def f(E, o=out, l=lhsT, r=rhs, st=start, sp=stop, i=inc):
            ins = E.matmul(o, l, r, start=st, stop=sp)
            if i:
                ins.then_inc(self.sems["pe"], 1)
        self.raw("tensor", f)
        if inc:
            self.pe += 1
            return self.pe

    def tr(self, out, in_, ident, inc=False):
        def f(E, o=out, x=in_, idn=ident, i=inc):
            ins = E.transpose(o, x, idn)
            if i:
                ins.then_inc(self.sems["pe"], 1)
        self.raw("tensor", f)
        if inc:
            self.pe += 1
            return self.pe

    def sc(self, method, *a, inc=True, selfwait=False, **k):
        if selfwait:
            self.wait("scalar", "act", self.act)

        def f(E, m=method, aa=a, kk=k):
            getattr(E, m)(*aa, **kk).then_inc(self.sems["act"], 1)
        self.raw("scalar", f)
        self.act += 1
        return self.act

    def gp(self, method, *a, **k):
        self.wait("gpsimd", "gp", self.gpn)

        def f(E, m=method, aa=a, kk=k):
            getattr(E, m)(*aa, **kk).then_inc(self.sems["gp"], 1)
        self.raw("gpsimd", f)
        self.gpn += 1
        return self.gpn

    def ve(self, method, *a, **k):
        self.wait("vector", "dve", self.dve)

        def f(E, m=method, aa=a, kk=k):
            getattr(E, m)(*aa, **kk).then_inc(self.sems["dve"], 1)
        self.raw("vector", f)
        self.dve += 1
        return self.dve

    def dma(self, q, out, in_):
        base = "dg" if q == "gpsimd" else "ds"
        lane = base + str(self.rr[q] % self.NL)
        self.rr[q] += 1
        k = self.lane_cnt[lane]
        if k > 0:
            self.raw(q, lambda E, n=lane, v=16 * k: E.wait_ge(self.sems[n], v))

        def f(E, o=out, x=in_, n=lane):
            E.dma_start(out=o, in_=x).then_inc(self.sems[n], 16)
        self.raw(q, f)
        self.lane_cnt[lane] = k + 1
        return (lane, 16 * (k + 1))

    def wait_tok(self, eng, tok):
        if not tok:
            return
        if isinstance(tok, list):
            for t in tok:
                self.wait_tok(eng, t)
            return
        self.wait(eng, tok[0], tok[1])

    def fence_toks(self, q):
        base = "dg" if q == "gpsimd" else "ds"
        return [(base + str(l), 16 * self.lane_cnt[base + str(l)])
                for l in range(self.NL) if self.lane_cnt[base + str(l)] > 0]

    @staticmethod
    def tok_max(a, b):
        if not a:
            return b
        if not b:
            return a
        if isinstance(a, list) or isinstance(b, list):
            aa = a if isinstance(a, list) else [a]
            bb = b if isinstance(b, list) else [b]
            return aa + bb
        if a[0] == b[0]:
            return a if a[1] >= b[1] else b
        return [a, b]

    def join(self):
        pe, act, dve = self.pe, self.act, self.dve
        self.wait("tensor", "act", act)
        self.wait("tensor", "dve", dve)
        self.wait("scalar", "pe", pe)
        self.wait("scalar", "dve", dve)
        self.wait("vector", "pe", pe)
        self.wait("vector", "act", act)


def build_nc():
    nc = bass.Bass("TRN2", target_bir_lowering=False)

    x_d = nc.dram_tensor("x", [128, DC, TOK], F32, kind="ExternalInput").ap()
    wr_d = nc.dram_tensor("wr", [128, DC, 64], F32, kind="ExternalInput").ap()
    at_d = nc.dram_tensor("at", [8, 128, DC, 128], F32, kind="ExternalInput").ap()
    bt_d = nc.dram_tensor("bt", [DC, 128, 8, 128], F32, kind="ExternalInput").ap()
    wq_d = nc.dram_tensor("wq", [4, 128, DC, 128], F32, kind="ExternalInput").ap()
    wk_d = nc.dram_tensor("wk", [128, DC, 128], F32, kind="ExternalInput").ap()
    wv_d = nc.dram_tensor("wv", [128, DC, 128], F32, kind="ExternalInput").ap()
    wo_d = nc.dram_tensor("wo", [DC, 128, DC, 128], F32, kind="ExternalInput").ap()
    bqc_d = nc.dram_tensor("bqc", [128, 4], F32, kind="ExternalInput").ap()
    bkc_d = nc.dram_tensor("bkc", [128, 1], F32, kind="ExternalInput").ap()
    bvc_d = nc.dram_tensor("bvc", [128, 1], F32, kind="ExternalInput").ap()
    boc_d = nc.dram_tensor("boc", [128, 32], F32, kind="ExternalInput").ap()
    cvec_d = nc.dram_tensor("cvec", [1, 8], F32, kind="ExternalInput").ap()
    y_d = nc.dram_tensor("y", [D, TOK], F32, kind="ExternalOutput").ap()
    dbg_d = (nc.dram_tensor("dbg", [128, 32768], U8, kind="ExternalOutput").ap()
             if DEV_STOP == "p2" else None)

    x1_d = nc.dram_tensor("x1buf", [D, TOK], F32).ap()
    a16_d = nc.dram_tensor("a16buf", [D, KR], BF16).ap()
    b16_d = nc.dram_tensor("b16buf", [KR, D], BF16).ap()
    wo16_d = nc.dram_tensor("wo16buf", [D, D], BF16).ap()
    x2_d = nc.dram_tensor("x2buf", [D, TOK], F32).ap()
    gin_d = nc.dram_tensor("ginbuf", [D, TOK], BF16).ap()
    gouts = [nc.dram_tensor(f"goutbuf{g}", [8 * 1024, TOK], BF16,
                            addr_space="Shared").ap() for g in range(4)]
    gin2_d = nc.dram_tensor("gin2buf", [2 * HD, TOK], BF16).ap()
    gout2_d = nc.dram_tensor("gout2buf", [8 * 2 * HD, TOK], BF16,
                             addr_space="Shared").ap()

    id16_h = nc.inline_tensor(np.eye(128, dtype=BF), "id16c")
    onescol_h = nc.inline_tensor(np.ones((128, 1), dtype=BF), "onescolc")
    onesrow_h = nc.inline_tensor(np.ones((1, 128), dtype=BF), "onesrowc")
    expand_np = np.zeros((64, 1024), dtype=BF)
    for krx in range(1024):
        expand_np[krx // 16, krx] = 1.0
    expand_h = nc.inline_tensor(expand_np, "expandc")

    p = Pl()
    es = ExitStack()

    def sbt(name, shape, dtype):
        return es.enter_context(nc.sbuf_tensor(name, shape, dtype))

    xbu = sbt("xbu", [128, 65536], U8)
    ybu = sbt("ybu", [128, 32768], U8)
    ar1 = sbt("ar1", [128, 16384], U8)
    ar2 = sbt("ar2", [128, 16384], U8)
    ar3 = sbt("ar3", [128, 8192], U8)
    arM = sbt("arM", [128, 4096], U8)

    def carve(t, byte_off, dtype, shape):
        nb = int(np.prod(shape[1:])) * mybir.dt.size(dtype)
        ap = t[:, byte_off:byte_off + nb].bitcast(dtype)
        if len(shape) > 2:
            kw = {f"d{i}": shape[i] for i in range(1, len(shape))}
            dims = " ".join(f"d{i}" for i in range(1, len(shape)))
            ap = ap.rearrange(f"p ({dims}) -> p {dims}", **kw)
        if shape[0] != 128:
            ap = ap[0:shape[0]]
        return ap

    xb = carve(xbu, 0, BF16, [128, DC, TOK])
    Qv = carve(xbu, 0, BF16, [128, 2, 8192])
    lnring = [carve(xbu, 32768 + j * 16384, BF16, [128, DC, 256]) for j in range(2)]
    Yt = carve(ybu, 0, BF16, [128, 32, 2, 128, 2])
    YtC = carve(ybu, 0, BF16, [128, 32, 512])
    Vfm = carve(ybu, 0, BF16, [128, 8192])
    Gexp = carve(ar1, 0, BF16, [128, 8, TOK])
    Ksb = carve(ar1, 0, BF16, [128, 8192])
    hG = carve(ar2, 0, BF16, [128, 8, TOK])
    Vtm = carve(ar2, 0, BF16, [128, 8192])
    Ering = carve(ar3, 0, BF16, [128, 4, 1024])
    O16 = carve(arM, 0, BF16, [128, 512])
    qsq16 = carve(arM, 1024, BF16, [128, 512])
    mrg16 = carve(arM, 2048, BF16, [128, 1024])
    gmx8 = sbt("gmx8", [128, 64], F32)
    O16b = carve(arM, 1024, BF16, [128, 512])
    sqA = carve(arM, 0, BF16, [128, 1024])
    sqB = carve(arM, 2048, BF16, [128, 1024])
    sc_tm = carve(arM, 0, F32, [128, 512])
    sc_zap = carve(arM, 2048, F32, [128, 512])
    m_bc = carve(arM, 0, F32, [128, TOK])

    r_bc = sbt("r_bc", [128, TOK], F32)[:, :]
    wbuf = [sbt("wbuf0", [128, 4096], BF16), sbt("wbuf1", [128, 4096], BF16)]
    xpg = sbt("xpg", [128, 4, 512], F32)[:, :, :]
    xout = sbt("xout", [128, 4, 512], F32)[:, :, :]
    xsq = sbt("xsqr", [128, 2, TOK], BF16)[:, :, :]
    scr = sbt("scrf", [128, TOK], F32)[:, :]
    rows16 = sbt("rows16", [128, TOK], BF16)[:, :]
    rows5f = sbt("rows5f", [128, 512], F32)[:, :]
    rows5h = sbt("rows5h", [128, 512], BF16)[:, :]
    tinyf = sbt("tinyf", [128, 16], F32)[:, :]
    tiny16 = sbt("tiny16", [128, 8], BF16)[:, :]
    m8a = sbt("m8a", [128, 8], F32)[:, :]
    m8b = sbt("m8b", [128, 8], F32)[:, :]
    gneg = sbt("gneg", [128, 1], F32)[:, :]
    gsumr = sbt("gsumr", [128, 2], F32)[:, :]
    Gfm16 = sbt("Gfm16", [64, TOK], BF16)[0:64, :]
    sumrow = sbt("sumrow", [1, KR], BF16)
    rhis = sbt("rhis", [1, TOK], BF16)[0:1, :]
    rlos = sbt("rlos", [1, TOK], BF16)[0:1, :]
    lo5 = sbt("lo5", [1, 512], BF16)[0:1, :]
    wrsum = sbt("wrsum", [1, 64], BF16)
    negC2h = sbt("negC2h", [128, 2], F32)
    id16 = sbt("id16s", [128, 128], BF16)
    ones_c = sbt("ones_cs", [128, 1], BF16)
    ones_r = sbt("ones_rs", [1, 128], BF16)
    bqc = sbt("bqc_s", [128, 4], F32)
    bkc = sbt("bkc_s", [128, 1], F32)
    bvc = sbt("bvc_s", [128, 1], F32)
    boc = sbt("boc_s", [128, 32], F32)
    cvec = sbt("cvec_s", [1, 8], F32)

    psA = es.enter_context(nc.psum_tensor("psA", [128, 1024], F32))[:, :]
    psB = es.enter_context(nc.psum_tensor("psB", [128, 1024], F32))[:, :]
    psDE = es.enter_context(nc.psum_tensor("psDE", [128, 1024], F32))[:, :]
    psC = es.enter_context(nc.psum_tensor("psC", [128, 512], F32))[:, :]
    psT = es.enter_context(nc.psum_tensor("psT", [128, 1024], BF16))[:, :]
    psD = psDE[:, 0:512]
    psE = psDE[:, 512:1024]

    semnames = ["pe", "act", "dve", "cc", "gp"]
    for b in ("dg", "ds"):
        for l in range(Pl.NL):
            semnames.append(b + str(l))
    for sname in semnames:
        p.sems[sname] = es.enter_context(nc.semaphore("sem_" + sname))

    ctoks = []
    for dst, src in ((id16[:, :], id16_h.ap()), (ones_c[:, :], onescol_h.ap()),
                     (ones_r[:, :], onesrow_h.ap())):
        ctoks.append(p.dma("gpsimd", dst, src))
    for dst, src in ((bqc[:, :], bqc_d), (bkc[:, :], bkc_d), (bvc[:, :], bvc_d),
                     (boc[:, :], boc_d), (cvec[:, :], cvec_d)):
        ctoks.append(p.dma("sync", dst, src))
    for e in ("tensor", "scalar", "vector"):
        p.wait_tok(e, ctoks)

    accP = [xpg[:, 0:2, :].rearrange("p a b -> p (a b)"),
            xpg[:, 2:4, :].rearrange("p a b -> p (a b)")]
    emask = sbt("emask", [128, 512], BF16)[:, :]
    Gtm16 = rows5h
    expand16 = rows5f.bitcast(BF16)[0:64, :]
    st_m = carve(ar2, 0, F32, [1, TOK])
    st_ss = carve(ar2, 4096, F32, [1, TOK])
    st_msq = carve(ar2, 8192, F32, [1, TOK])
    st_var = carve(ar2, 12288, F32, [1, TOK])
    st_lnv = carve(ar1, 0, F32, [1, TOK])
    st_rstd = carve(ar1, 4096, F32, [1, TOK])
    scfm = carve(ar2, 0, F32, [64, TOK])
    scr64 = scr[0:64, :]
    schi = xsq[0:64, 0, :]
    sclo = xsq[0:64, 1, :]
    kvloc = xsq[:, :, :]

    pg_store = [None, None, None, None]
    pg_load = [None, None, None, None]
    pg_dve = [0, 0, 0, 0]

    def page_load(idx, src_dram, dc, col0):
        sl = idx % 4
        p.wait("sync", "dve", pg_dve[sl])
        if src_dram is x_d:
            srcap = x_d[:, dc, col0:col0 + 512]
        else:
            srcap = src_dram[dc * 128:(dc + 1) * 128, col0:col0 + 512]
        tok = p.dma("sync", xpg[:, sl, :], srcap)
        pg_load[sl] = tok
        return tok

    def page_evac_store(idx, ps_ap, dst_dram, dc, col0, bias_col=None):
        sl = idx % 4
        p.wait_tok("vector", Pl.tok_max(pg_load[sl], pg_store[sl]))
        if bias_col is None:
            n = p.ve("tensor_tensor", xout[:, sl, :], ps_ap, xpg[:, sl, :], ALU.add)
        else:
            n = p.ve("scalar_tensor_tensor", xout[:, sl, :], ps_ap, bias_col,
                     xpg[:, sl, :], ALU.add, ALU.add)
        pg_dve[sl] = n
        p.wait("sync", "dve", n)
        tok = p.dma("sync", dst_dram[dc * 128:(dc + 1) * 128, col0:col0 + 512],
                    xout[:, sl, :])
        pg_store[sl] = tok
        return tok

    wslot_free = [0, 0]
    wslot_wb = [None, None]

    def load_w(slot, dram_view, nchunks, width, queue="gpsimd", wb_view=None):
        p.wait(queue, "pe", wslot_free[slot])
        p.wait_tok(queue, wslot_wb[slot])
        wslot_wb[slot] = None
        view = wbuf[slot][:, 0:nchunks * width].rearrange("p (c w) -> p c w", w=width)
        tok = p.dma(queue, view, dram_view)
        if wb_view is not None:
            p.wait_tok("sync", tok)
            wslot_wb[slot] = p.dma("sync", wb_view, view)
        return tok, view

    def ln_rows(sxA, sxB, sq, want_mbc):
        p.sc("activation", st_m[:, 0:512], sxA, AF.Copy, scale=INVD)
        p.sc("activation", st_m[:, 512:1024], sxB, AF.Copy, scale=INVD)
        am = p.sc("activation", st_ss[:, :], sq, AF.Copy, scale=INVD)
        a1 = p.sc("activation", st_msq[:, :], st_m[:, :], AF.Square,
                  selfwait=True)
        p.wait("vector", "act", a1)
        nvar = p.ve("tensor_tensor", st_var[:, :], st_ss[:, :], st_msq[:, :],
                    ALU.subtract)
        p.wait("scalar", "dve", nvar)
        p.sc("activation", st_lnv[:, :], st_var[:, :], AF.Ln, bias=cvec[0:1, 0:1])
        a2 = p.sc("activation", st_rstd[:, :], st_lnv[:, :], AF.Exp, scale=-0.5,
                  selfwait=True)
        p.ve("tensor_scalar", rows16[0:1, :], st_m[:, :], -1.0, None, ALU.mult)
        p.wait("vector", "act", a2)
        p.ve("tensor_copy", rhis, st_rstd[:, :])
        p.ve("tensor_tensor", scr[0:1, :], st_rstd[:, :], rhis, ALU.subtract)
        nlo = p.ve("tensor_copy", rlos, scr[0:1, :])
        p.wait("tensor", "dve", nlo)
        p.wait("tensor", "act", am)
        rbc_act = 0
        for nch in range(2):
            cs = slice(nch * 512, (nch + 1) * 512)
            p.mm(psB[:, cs], ones_r[:, :], rhis[:, cs], start=True, stop=False)
            bcp = p.mm(psB[:, cs], ones_r[:, :], rlos[:, cs], start=False,
                       stop=True, inc=True)
            p.wait("scalar", "pe", bcp)
            rbc_act = p.sc("copy", r_bc[:, cs], psB[:, cs])
        mbc_act = 0
        if want_mbc:
            p.wait("tensor", "act", rbc_act)
            p.wait("vector", "pe", p.pe)
            for nch in range(2):
                cs = slice(nch * 512, (nch + 1) * 512)
                p.wait("vector", "pe", p.pe)
                p.ve("tensor_copy", rhis[:, 0:512], st_m[:, cs])
                p.ve("tensor_tensor", scr[0:1, 0:512], st_m[:, cs],
                     rhis[:, 0:512], ALU.subtract)
                nl = p.ve("tensor_copy", lo5, scr[0:1, 0:512])
                p.wait("tensor", "dve", nl)
                if nch == 1:
                    p.wait("tensor", "act", mbc_act)
                p.mm(psB[:, cs], ones_r[:, :], rhis[:, 0:512], start=True, stop=False)
                bcp = p.mm(psB[:, cs], ones_r[:, :], lo5, start=False,
                           stop=True, inc=True)
                p.wait("scalar", "pe", bcp)
                mbc_act = p.sc("copy", m_bc[:, cs], psB[:, cs])
        return rbc_act, mbc_act

    def ln_stats(xb_dg, want_mbc):
        if not isinstance(xb_dg, list):
            xb_dg = [xb_dg]
        last = 0
        slot_pe = [0, 0]
        for dc in range(DC):
            rg = dc % 2
            gi = min(dc // 8, len(xb_dg) - 1)
            p.wait_tok("vector", xb_dg[gi])
            p.wait_tok("tensor", xb_dg[gi])
            p.wait("vector", "pe", slot_pe[rg])
            nd = p.ve("tensor_tensor", xsq[:, rg, :], xb[:, dc, :], xb[:, dc, :],
                      ALU.mult)
            p.mm(psC[0:1, :], ones_c[:, :], xb[:, dc, 0:512],
                 start=(dc == 0), stop=(dc == DC - 1))
            p.mm(psD[0:1, :], ones_c[:, :], xb[:, dc, 512:1024],
                 start=(dc == 0), stop=(dc == DC - 1))
            p.wait("tensor", "dve", nd)
            p.mm(psA[0:1, 0:512], ones_c[:, :], xsq[:, rg, 0:512],
                 start=(dc == 0), stop=(dc == DC - 1))
            last = p.mm(psA[0:1, 512:1024], ones_c[:, :], xsq[:, rg, 512:1024],
                        start=(dc == 0), stop=(dc == DC - 1), inc=True)
            slot_pe[rg] = last
        p.wait("scalar", "pe", last)
        return ln_rows(psC[0:1, :], psD[0:1, :], psA[0:1, :], want_mbc)

    def plan_ffn(src_view, dst_dram, resid_dram, cached, prev_fences=None,
                 fuse_ln1=False):
        p.join()
        p.wait("gpsimd", "pe", p.pe)
        p.wait("gpsimd", "act", p.act)
        p.wait("gpsimd", "dve", p.dve)
        xb_dg = []
        for g in range(4):
            if prev_fences is not None:
                p.wait_tok("gpsimd", prev_fences[g])
            xb_dg.append(p.dma("gpsimd", xb[:, g * 8:(g + 1) * 8, :],
                               src_view[:, g * 8:(g + 1) * 8, :]))
        wr_dg, wrv = load_w(0, wr_d, DC, 64)
        exp_dg = p.dma("gpsimd", expand16, expand_h.ap()[0:64, :])
        rbc_act, _ = ln_stats(xb_dg, want_mbc=False)

        def a_src(kc):
            return (at_d[kc], "gpsimd", None)

        def b_src(dc):
            return (bt_d[dc], "gpsimd", None)

        p.wait_tok("tensor", xb_dg[-1] if isinstance(xb_dg, list) else xb_dg)
        p.wait_tok("tensor", wr_dg)
        wr_pe = 0
        for dc in range(DC):
            wr_pe = p.mm(psE[0:1, 0:64], ones_c[:, :], wrv[:, dc, :],
                         start=(dc == 0), stop=(dc == DC - 1), inc=(dc == DC - 1))
        p.wait("scalar", "pe", wr_pe)
        awr = p.sc("copy", wrsum[0:1, :], psE[0:1, 0:64], inc=True)
        p.wait("tensor", "act", awr)
        p.wait("tensor", "act", rbc_act)
        p.wait("vector", "act", rbc_act)
        router_dve = 0
        for nch in range(2):
            cs = slice(nch * 512, (nch + 1) * 512)
            for dc in range(DC):
                p.mm(psB[0:64, cs], wrv[:, dc, :], xb[:, dc, cs],
                     start=(dc == 0), stop=False)
            rt = p.mm(psB[0:64, cs], wrsum[0:1, :], rows16[0:1, cs],
                      start=False, stop=True, inc=True)
            p.wait("vector", "pe", rt)
            router_dve = p.ve("tensor_tensor", scfm[:, cs], psB[0:64, cs],
                              r_bc[0:64, cs], ALU.mult)
        wslot_free[0] = p.pe

        hg_dve = [0] * 8
        hpe = {}
        ffn_state = dict(ak_prev=0)

        def h_mms(kc):
            dview, q, wb = a_src(kc)
            slot = kc % 2
            a_dg, av = load_w(slot, dview, DC, 128, queue=q, wb_view=wb)
            ps = psA if kc % 2 == 0 else psB
            p.wait_tok("tensor", a_dg)
            if kc >= 2:
                p.wait("tensor", "dve", hg_dve[kc - 2])
            elif kc == 1:
                p.wait("tensor", "dve", router_dve)
            p.wait("tensor", "act", ffn_state["ak_prev"])
            ap_ = 0
            for dc in range(DC):
                ap_ = p.mm(psE[0:1, 0:128], ones_c[:, :], av[:, dc, :],
                           start=(dc == 0), stop=(dc == DC - 1), inc=(dc == DC - 1))
            p.wait("scalar", "pe", ap_)
            aas = p.sc("copy", sumrow[0:1, kc * 128:(kc + 1) * 128],
                       psE[0:1, 0:128])
            ffn_state["ak_prev"] = aas
            p.wait("tensor", "act", aas)
            for nch in range(2):
                cs = slice(nch * 512, (nch + 1) * 512)
                for dc in range(DC):
                    p.mm(ps[:, cs], av[:, dc, :], xb[:, dc, cs],
                         start=(dc == 0), stop=False)
                hpe[(kc, nch)] = p.mm(ps[:, cs],
                                      sumrow[0:1, kc * 128:(kc + 1) * 128],
                                      rows16[0:1, cs], start=False, stop=True,
                                      inc=True)
            wslot_free[slot] = p.pe

        def hg_evac(kc, ge_act):
            ps = psA if kc % 2 == 0 else psB
            for nch in range(2):
                cs = slice(nch * 512, (nch + 1) * 512)
                p.wait("vector", "pe", hpe[(kc, nch)])
                p.wait("vector", "act", ge_act[kc * 2 + nch])
                hg_dve[kc] = p.ve("tensor_tensor", hG[:, kc, cs], ps[:, cs],
                                  Gexp[:, kc, cs], ALU.mult)

        h_mms(0)
        h_mms(1)

        p.ve("tensor_copy", schi[:, :], scfm[:, :])
        p.ve("tensor_tensor", scr64[:, :], scfm[:, :], schi[:, :], ALU.subtract)
        nlo = p.ve("tensor_copy", sclo[:, :], scr64[:, :])
        p.wait("tensor", "dve", nlo)
        tpe = 0
        for j in range(8):
            p.tr(psT[:, j * 64:(j + 1) * 64], schi[:, j * 128:(j + 1) * 128],
                 id16[0:64, 0:64])
        for j in range(8):
            tpe = p.tr(psT[:, 512 + j * 64:512 + (j + 1) * 64],
                       sclo[:, j * 128:(j + 1) * 128], id16[0:64, 0:64], inc=(j == 7))
        p.wait("scalar", "pe", tpe)
        ahi = p.sc("copy", sc_tm[:, :], psT[:, 0:512])
        p.wait("vector", "act", ahi)
        p.ve("tensor_tensor", sc_tm[:, :], sc_tm[:, :], psT[:, 512:1024], ALU.add)
        for j in range(8):
            t = slice(j * 64, (j + 1) * 64)
            p.ve("max", gmx8[:, j * 8:(j + 1) * 8], sc_tm[:, t])
            p.ve("match_replace", out=sc_zap[:, t],
                 in_to_replace=gmx8[:, j * 8:(j + 1) * 8],
                 in_values=sc_tm[:, t], imm_value=-1e30)
            p.ve("max", m8b[:, :], sc_zap[:, t])
            p.ve("match_replace", out=scr[:, 512 + j * 64:512 + (j + 1) * 64],
                 in_to_replace=m8b[:, :],
                 in_values=sc_zap[:, t], imm_value=-1e30)
        gseg = lambda ap: ap.rearrange("p (s o) -> p s o", s=8)
        nsub = p.ve("tensor_tensor", gseg(scr[:, 0:512]), gseg(sc_tm[:, :]),
                    gmx8[:, :].rearrange("p (s o) -> p s o", o=8)[:, :, 0:1]
                    .broadcast_to([128, 8, 64]), ALU.subtract)
        p.wait("scalar", "dve", nsub)
        ag = p.sc("activation", sc_zap[:, :], scr[:, 0:512], AF.Exp)
        p.ve("tensor_scalar", emask[:, :], scr[:, 512:1024], -1e29, None,
             ALU.is_le)
        p.wait("vector", "act", ag)
        p.ve("tensor_tensor", sc_tm[:, :], sc_zap[:, :], emask[:, :], ALU.mult)
        p.ve("reduce_sum", m8b[:, :].rearrange("p (s o) -> p s o", o=1),
             gseg(sc_tm[:, :]), axis=AX)
        p.ve("reciprocal", m8a[:, :], m8b[:, :])
        ng = p.ve("tensor_tensor", gseg(Gtm16[:, :]), gseg(sc_tm[:, :]),
                  m8a[:, :].rearrange("p (s o) -> p s o", o=1)
                  .broadcast_to([128, 8, 64]), ALU.mult)
        p.wait("tensor", "dve", ng)
        gt = 0
        for j in range(8):
            gt = p.tr(psT[0:64, j * 128:(j + 1) * 128], Gtm16[:, j * 64:(j + 1) * 64],
                      id16[:, :], inc=(j == 7))
        p.wait("vector", "pe", gt)
        ngf = 0
        for j in range(8):
            ngf = p.ve("tensor_tensor", Gfm16[:, j * 128:(j + 1) * 128],
                       psT[0:64, j * 128:(j + 1) * 128],
                       r_bc[0:64, j * 128:(j + 1) * 128], ALU.mult)
        p.wait("tensor", "dve", ngf)
        p.wait_tok("tensor", exp_dg)
        ge_act = []
        for kc in range(8):
            for nch in range(2):
                ps = psC if (kc * 2 + nch) % 2 == 0 else psD
                if len(ge_act) >= 2:
                    p.wait("tensor", "act", ge_act[-2])
                ge = p.mm(ps[:, 0:512], expand16[:, kc * 128:(kc + 1) * 128],
                          Gfm16[:, nch * 512:(nch + 1) * 512], inc=True)
                p.wait("scalar", "pe", ge)
                ge_act.append(p.sc("copy", Gexp[:, kc, nch * 512:(nch + 1) * 512],
                                   ps[:, 0:512]))
        hg_evac(0, ge_act)
        hg_evac(1, ge_act)
        for kc in range(2, 8):
            h_mms(kc)
            hg_evac(kc, ge_act)
        p.wait("tensor", "act", p.act)
        out_ps_dve = [0, 0]
        last_store = 0
        bsub_free = [wslot_free[0], wslot_free[1]] * 4
        bsub_tok = [None] * 8

        def bload(dc):
            sub = dc % 8
            view = wbuf[sub % 2][:, (sub // 2) * 1024:(sub // 2 + 1) * 1024]
            view = view.rearrange("p (c w) -> p c w", w=128)
            p.wait("gpsimd", "pe", bsub_free[sub])
            bsub_tok[sub] = p.dma("gpsimd", view, bt_d[dc])
            return view

        bviews = {}
        for dc in range(min(8, DC)):
            bviews[dc] = bload(dc)
        oct_fences = []
        pg_cast = [0, 0, 0, 0]
        xsq_pe = [0, 0]
        xbcast_act = 0
        stat_pe = 0
        for dc in range(DC):
            sub = dc % 8
            bv = bviews.pop(dc)
            ps = psC if dc % 2 == 0 else psD
            p.wait_tok("tensor", bsub_tok[sub])
            if dc == 0:
                p.wait("tensor", "dve", hg_dve[7])
            for nch in range(2):
                idx = dc * 2 + nch
                page_load(idx, resid_dram, dc, nch * 512)
                p.wait("tensor", "dve", out_ps_dve[dc % 2])
                op = 0
                for kc in range(8):
                    op = p.mm(ps[:, 0:512], bv[:, kc, :],
                              hG[:, kc, nch * 512:(nch + 1) * 512],
                              start=(kc == 0), stop=(kc == 7), inc=(kc == 7))
                p.wait("vector", "pe", op)
                sl = idx % 4
                p.wait("vector", "act", pg_cast[sl])
                last_store = page_evac_store(idx, ps[:, 0:512], dst_dram, dc,
                                             nch * 512)
                out_ps_dve[dc % 2] = pg_dve[idx % 4]
                if fuse_ln1:
                    cs = slice(nch * 512, (nch + 1) * 512)
                    p.wait("scalar", "dve", pg_dve[sl])
                    xbcast_act = p.sc("copy", xb[:, dc, cs], xout[:, sl, :])
                    pg_cast[sl] = xbcast_act
                    p.wait("vector", "act", xbcast_act)
                    p.wait("vector", "pe", xsq_pe[idx % 2])
                    nq = p.ve("tensor_tensor", xsq[:, idx % 2, 0:512],
                              xb[:, dc, cs], xb[:, dc, cs], ALU.mult)
                    p.wait("tensor", "dve", nq)
                    p.mm(psA[0:1, cs], ones_c[:, :], xb[:, dc, cs],
                         start=(dc == 0), stop=(dc == DC - 1))
                    stat_pe = p.mm(psB[0:1, cs], ones_c[:, :],
                                   xsq[:, idx % 2, 0:512],
                                   start=(dc == 0), stop=(dc == DC - 1),
                                   inc=True)
                    xsq_pe[idx % 2] = stat_pe
            bsub_free[sub] = p.pe
            if dc + 8 < DC:
                bviews[dc + 8] = bload(dc + 8)
            if dc % 8 == 7:
                oct_fences.append(p.fence_toks("sync"))
        wslot_free[0] = p.pe
        wslot_free[1] = p.pe
        if fuse_ln1:
            p.wait("scalar", "pe", stat_pe)
            rbc_act, mbc_act = ln_rows(psA[0:1, 0:512], psA[0:1, 512:1024],
                                       psB[0:1, :], want_mbc=True)
            return last_store, oct_fences, rbc_act, mbc_act, xbcast_act
        return last_store, oct_fences

    def plan_ln1_gather(rbc_act, mbc_act, xb_done_act):
        p.join()
        p.wait("gpsimd", "pe", p.pe)
        p.wait("vector", "act", max(rbc_act, mbc_act, xb_done_act))
        p.wait("vector", "pe", p.pe)
        apply_dve = [0] * DC
        wk_dg, wkv = load_w(0, wk_d, DC, 128)
        wv_dg, wvv = load_w(1, wv_d, DC, 128)
        for dc in range(DC):
            p.ve("tensor_tensor", scr[:, :], xb[:, dc, :], m_bc[:, :], ALU.subtract)
            apply_dve[dc] = p.ve("tensor_tensor", xb[:, dc, :], scr[:, :], r_bc[:, :],
                                 ALU.mult)
        for g in range(4):
            p.wait("sync", "dve", apply_dve[g * 8 + 7])
            p.dma("sync",
                  gin_d.rearrange("(c p) n -> p c n", p=128)[:, g * 8:(g + 1) * 8, :],
                  xb[:, g * 8:(g + 1) * 8, :])
            p.wait_tok("gpsimd", p.fence_toks("sync"))

            def collg(E, gg=g):
                E.collective_compute(
                    "AllGather", ALU.bypass,
                    replica_groups=[list(range(8))],
                    ins=[gin_d[gg * 1024:(gg + 1) * 1024, :].opt()],
                    outs=[gouts[gg].opt()],
                ).then_inc(p.sems["cc"], 1)
            p.raw("gpsimd", collg)
            p.cc += 1
        kv_act = 0
        p.wait_tok("tensor", wk_dg)
        p.wait_tok("tensor", wv_dg)
        for which in range(2):
            wvw = wkv if which == 0 else wvv
            bias = bkc if which == 0 else bvc
            ps = psC if which == 0 else psD
            for nch in range(2):
                cs = slice(nch * 512, (nch + 1) * 512)
                if which == 1 or nch == 1:
                    p.wait("tensor", "act", kv_act)
                kp = 0
                for dc in range(DC):
                    p.wait("tensor", "dve", apply_dve[dc])
                    kp = p.mm(ps[:, 0:512], wvw[:, dc, :], xb[:, dc, cs],
                              start=(dc == 0), stop=(dc == DC - 1),
                              inc=(dc == DC - 1))
                p.wait("scalar", "pe", kp)
                kv_act = p.sc("activation", kvloc[:, which, cs], ps[:, 0:512],
                              AF.Identity, bias=bias[:, 0:1])
        wslot_free[0] = p.pe
        wslot_free[1] = p.pe
        p.wait("sync", "act", kv_act)
        p.dma("sync", gin2_d[0:HD, :], kvloc[:, 0, :])
        p.dma("sync", gin2_d[HD:2 * HD, :], kvloc[:, 1, :])
        p.wait_tok("gpsimd", p.fence_toks("sync"))

        def coll2(E):
            E.collective_compute(
                "AllGather", ALU.bypass,
                replica_groups=[list(range(8))],
                ins=[gin2_d.opt()],
                outs=[gout2_d.opt()],
            ).then_inc(p.sems["cc"], 1)
        p.raw("gpsimd", coll2)
        p.cc += 1
        return p.cc

    att = dict(last_exp=[0, 0, 0], eslot_pe=[0, 0, 0, 0], eslot_dve=[0, 0, 0, 0],
               psc_free=0, psob_free=0)

    def _av2(kt, exp_of_kt, psOb):
        p.wait("tensor", "act", exp_of_kt)
        if kt == 0:
            p.wait("tensor", "act", att["psc_free"])
            p.wait("tensor", "dve", att["psob_free"])
        p.mm(psC[:, :], Vtm[:, kt * 128:(kt + 1) * 128],
             Ering[:, kt % 4, 0:512],
             start=(kt == 0), stop=(kt == 63))
        cnt = p.mm(psOb, Vtm[:, kt * 128:(kt + 1) * 128],
                   Ering[:, kt % 4, 512:1024],
                   start=(kt == 0), stop=(kt == 63), inc=True)
        att["eslot_pe"][kt % 4] = cnt
        return cnt

    def plan_attention(cc_ready):
        p.join()
        p.wait("sync", "pe", p.pe)
        p.wait("sync", "act", p.act)
        p.wait("sync", "dve", p.dve)
        km_dve = 0
        ps_ring = (psA, psB, psDE)
        x2_last_store = 0
        x2_fen = []
        for hp in range(2):
            wdg = [None, None]
            for oc in range(2):
                wdg[oc], _ = load_w(oc, wq_d[hp * 2 + oc], DC, 128)
            lnld = [0, 0]
            q_act = 0
            qp_act4 = [0, 0, 0, 0]
            p.wait("scalar", "pe", p.pe)
            for qc in range(32):
                rg = qc % 2
                rr = (qc * 256) // 1024
                c0 = (qc * 256) % 1024
                p.wait("sync", "pe", lnld[rg])
                lcs = []
                for g in range(4):
                    p.wait("sync", "cc", g + 1)
                    lcs.append(p.dma(
                        "sync", lnring[rg][:, g * 8:(g + 1) * 8, :],
                        gouts[g][rr * 1024:(rr + 1) * 1024,
                                 c0:c0 + 256].rearrange(
                            "(c p) n -> p c n", p=128)))
                p.wait_tok("tensor", wdg)
                for oc in range(2):
                    pslot = (qc * 2 + oc) % 4
                    psq = (psA, psA, psB, psB)[pslot]
                    coff = (0, 512, 0, 512)[pslot]
                    p.wait("tensor", "act", qp_act4[pslot])
                    qp = 0
                    for dc in range(DC):
                        if dc % 8 == 0 and oc == 0:
                            p.wait_tok("tensor", lcs[dc // 8])
                        qp = p.mm(psq[:, coff:coff + 256],
                                  wbuf[oc][:, dc * 128:(dc + 1) * 128],
                                  lnring[rg][:, dc, :],
                                  start=(dc == 0), stop=(dc == DC - 1),
                                  inc=(dc == DC - 1))
                    p.wait("scalar", "pe", qp)
                    q_act = p.sc("activation", Qv[:, oc, qc * 256:(qc + 1) * 256],
                                 psq[:, coff:coff + 256], AF.Identity,
                                 bias=bqc[:, hp * 2 + oc:hp * 2 + oc + 1])
                    qp_act4[pslot] = q_act
                lnld[rg] = p.pe
            wslot_free[0] = p.pe
            wslot_free[1] = p.pe
            if hp == 0:
                p.wait("sync", "cc", cc_ready)
                kddr = []
                vddr = []
                for r in range(8):
                    kddr.append(p.dma("sync", Ksb[:, r * 1024:(r + 1) * 1024],
                                      gout2_d[r * 2 * HD:r * 2 * HD + HD, :]))
                for r in range(8):
                    vddr.append(p.dma("sync", Vfm[:, r * 1024:(r + 1) * 1024],
                                      gout2_d[r * 2 * HD + HD:(r + 1) * 2 * HD, :]))
                p.wait_tok("tensor", vddr)
                vt_act = 0
                for g in range(8):
                    if vt_act:
                        p.wait("tensor", "act", vt_act)
                    vt = 0
                    for t in range(8):
                        vt = p.tr(psT[:, t * 128:(t + 1) * 128],
                                  Vfm[:, (g * 8 + t) * 128:(g * 8 + t + 1) * 128],
                                  id16[:, :], inc=(t == 7))
                    p.wait("scalar", "pe", vt)
                    for t in range(8):
                        vt_act = p.sc("copy",
                                      Vtm[:, (g * 8 + t) * 128:(g * 8 + t + 1) * 128],
                                      psT[:, t * 128:(t + 1) * 128], inc=(t == 7))
                p.wait_tok("scalar", kddr)
                ak_prev = 0
                for i in range(16):
                    aq = p.sc("activation", qsq16[:, :], Ksb[:, i * 512:(i + 1) * 512],
                              AF.Square, inc=True)
                    p.wait("tensor", "act", aq)
                    p.wait("tensor", "act", ak_prev)
                    kp = p.mm(psE[0:1, 0:512], ones_c[:, :], qsq16[:, :], inc=True)
                    p.wait("scalar", "pe", kp)
                    ak_prev = p.sc("copy", rows5f[0:1, :], psE[0:1, 0:512], inc=True)
                    p.wait("vector", "act", ak_prev)
                    p.ve("reduce_max", tinyf[0:1, 0:1], rows5f[0:1, :], axis=AX)
                    if i == 0:
                        km_dve = p.ve("tensor_copy", tinyf[0:1, 2:3], tinyf[0:1, 0:1])
                    else:
                        km_dve = p.ve("tensor_tensor", tinyf[0:1, 2:3],
                                      tinyf[0:1, 2:3], tinyf[0:1, 0:1], ALU.max)
            wo_pre = [None, None]
            for mc in range(2):
                wo_pre[mc], _ = load_w(mc, wo_d[mc], DC, 128)
            wo_pre = [None, None]
            for mc in range(2):
                wo_pre[mc], _ = load_w(mc, wo_d[mc], DC, 128)
            p.wait_tok("vector", p.fence_toks("sync"))
            p.wait("vector", "pe", p.pe)
            p.wait("scalar", "act", q_act)
            p.wait("vector", "dve", km_dve)
            p.wait("vector", "pe", p.pe)
            p.wait("vector", "act", p.act)
            p.wait("tensor", "act", p.act)
            p.wait("tensor", "act", att["psc_free"])
            sqb = (sqA, sqB)
            scr_row = scr[0:1, 0:1024].rearrange("p (s o) -> p s o", s=2)
            seg_out = tinyf[0:1, 12:14].rearrange("p (s o) -> p s o", o=1)
            cp_prev = 0
            kp_hist = [0, 0]
            rm_prev = 0
            for i in range(16):
                sq = sqb[i % 2]
                p.wait("vector", "pe", kp_hist[i % 2])
                dv = p.ve("tensor_tensor",
                          sq.rearrange("p (s o) -> p s o", s=2),
                          Qv[:, :, i * 512:(i + 1) * 512],
                          Qv[:, :, i * 512:(i + 1) * 512], ALU.mult)
                p.wait("tensor", "dve", dv)
                p.wait("tensor", "act", cp_prev)
                p.mm(psD[0:1, 0:512], ones_c[:, :], sq[:, 0:512],
                     start=True, stop=True)
                kp = p.mm(psE[0:1, 0:512], ones_c[:, :], sq[:, 512:1024],
                          start=True, stop=True, inc=True)
                kp_hist[i % 2] = kp
                p.wait("scalar", "pe", kp)
                p.wait("scalar", "dve", rm_prev)
                p.sc("copy", scr[0:1, 0:512], psD[0:1, 0:512])
                cp_prev = p.sc("copy", scr[0:1, 512:1024], psE[0:1, 0:512],
                               inc=True)
                p.wait("vector", "act", cp_prev)
                p.ve("reduce_max", seg_out, scr_row, axis=AX)
                if i == 0:
                    rm_prev = p.ve("tensor_copy", tinyf[0:1, 6:8],
                                   tinyf[0:1, 12:14])
                else:
                    rm_prev = p.ve("tensor_tensor", tinyf[0:1, 6:8],
                                   tinyf[0:1, 6:8], tinyf[0:1, 12:14],
                                   ALU.max)
            anc = 0
            for hl2 in range(2):
                npr = p.ve("tensor_tensor", tinyf[0:1, 3:4],
                           tinyf[0:1, 6 + hl2:7 + hl2], tinyf[0:1, 2:3],
                           ALU.mult)
                p.wait("scalar", "dve", npr)
                p.sc("activation", tinyf[0:1, 4:5], tinyf[0:1, 3:4], AF.Ln)
                acx = p.sc("activation", tinyf[0:1, 5:6], tinyf[0:1, 4:5],
                           AF.Exp, scale=0.5, bias=cvec[0:1, 1:2],
                           selfwait=True)
                p.wait("vector", "act", acx)
                nc16 = p.ve("tensor_scalar", tiny16[0:1, 0:1],
                            tinyf[0:1, 5:6], -1.0, None, ALU.mult)
                p.wait("tensor", "dve", nc16)
                p.wait("tensor", "act", anc)
                ncp = p.mm(psC[:, 0:1], ones_r[:, :], tiny16[0:1, 0:1],
                           inc=True)
                p.wait("scalar", "pe", ncp)
                anc = p.sc("copy", negC2h[:, hl2:hl2 + 1], psC[:, 0:1])
            p.wait("scalar", "act", anc)
            for hl in range(2):
                psOb = psT.bitcast(F32)
                for qp in range(8):
                    qA = slice(qp * 1024, qp * 1024 + 512)
                    qB = slice(qp * 1024 + 512, qp * 1024 + 1024)
                    exp_cnt = [0] * 64
                    for kt in range(64):
                        bank = ps_ring[kt % 3]
                        p.wait("tensor", "act", att["last_exp"][kt % 3])
                        p.mm(bank[:, 0:512], Ksb[:, kt * 128:(kt + 1) * 128],
                             Qv[:, hl, qA], start=True, stop=True)
                        s_cnt = p.mm(bank[:, 512:1024],
                                     Ksb[:, kt * 128:(kt + 1) * 128],
                                     Qv[:, hl, qB], start=True, stop=True, inc=True)
                        p.wait("scalar", "pe", s_cnt)
                        p.wait("scalar", "pe", att["eslot_pe"][kt % 4])
                        p.wait("scalar", "dve", att["eslot_dve"][kt % 4])
                        exp_cnt[kt] = p.sc("activation", Ering[:, kt % 4, :],
                                           bank[:, :], AF.Exp, scale=S_ATTN,
                                           bias=negC2h[:, hl:hl + 1])
                        att["last_exp"][kt % 3] = exp_cnt[kt]
                        if kt >= 1:
                            _av2(kt - 1, exp_cnt[kt - 1], psOb)
                        if kt % 2 == 1:
                            kk = kt // 2
                            p.wait("vector", "act", exp_cnt[kt])
                            nm = p.ve("tensor_tensor", mrg16,
                                      Ering[:, (kt - 1) % 4, :],
                                      Ering[:, kt % 4, :], ALU.add)
                            att["eslot_dve"][(kt - 1) % 4] = nm
                            att["eslot_dve"][kt % 4] = nm
                            if kk == 0:
                                p.ve("tensor_copy", accP[0], mrg16)
                            else:
                                p.ve("tensor_tensor", accP[kk % 2],
                                     accP[(kk - 1) % 2], mrg16, ALU.add)
                    last_av = _av2(63, exp_cnt[63], psOb)
                    p.wait("scalar", "pe", last_av)
                    oca = p.sc("copy", O16[:, :], psC[:, :])
                    ocb = p.sc("copy", O16b[:, :], psOb)
                    p.wait("vector", "pe", last_av)
                    ncst = p.ve("tensor_copy", Ering[:, 0, :], accP[31 % 2])
                    att["eslot_dve"][0] = ncst
                    p.wait("tensor", "dve", ncst)
                    p.wait("tensor", "act", oca)
                    smm = 0
                    for t in range(4):
                        p.mm(psC[:, t:t + 1], Ering[:, 0, t * 128:(t + 1) * 128],
                             ones_c[:, :], start=True, stop=True)
                        smm = p.mm(psC[:, 4 + t:5 + t],
                                   Ering[:, 0, 512 + t * 128:512 + (t + 1) * 128],
                                   ones_c[:, :], start=True, stop=True,
                                   inc=(t == 3))
                    att["eslot_pe"][0] = smm
                    p.wait("scalar", "pe", smm)
                    nrc = 0
                    for i in range(8):
                        p.sc("activation", tinyf[:, 8 + i:9 + i], psC[:, i:i + 1],
                             AF.Ln)
                        nrc = p.sc("activation", tinyf[:, 8 + i:9 + i],
                                   tinyf[:, 8 + i:9 + i], AF.Exp, scale=-1.0,
                                   selfwait=True)
                    att["psc_free"] = nrc
                    p.wait("tensor", "act", ocb)
                    ot = 0
                    for t in range(4):
                        p.tr(psT[:, t * 128:(t + 1) * 128],
                             O16[:, t * 128:(t + 1) * 128], id16[:, :])
                    for t in range(4):
                        ot = p.tr(psT[:, 512 + t * 128:512 + (t + 1) * 128],
                                  O16b[:, t * 128:(t + 1) * 128], id16[:, :],
                                  inc=(t == 3))
                    p.wait("vector", "pe", ot)
                    p.wait("vector", "act", nrc)
                    nyt = 0
                    for half2 in range(2):
                        qc = qp * 2 + half2
                        par = qc // 8
                        for t in range(4):
                            cc = (qc % 8) * 4 + t
                            nyt = p.ve("tensor_scalar", Yt[:, cc, hl, :, par],
                                       psT[:, half2 * 512 + t * 128:
                                           half2 * 512 + (t + 1) * 128],
                                       tinyf[:, 8 + half2 * 4 + t:
                                             9 + half2 * 4 + t], None, ALU.mult)
                    att["psob_free"] = nyt
            if DEV_STOP == "p2" and hp == 0:
                p.wait("sync", "dve", p.dve)
                p.dma("sync", dbg_d, ybu[:, :])
            p.wait("tensor", "dve", p.dve)
            p.wait("sync", "dve", p.dve)
            wo_ps_dve = [0, 0]
            for mc in range(32):
                slot = mc % 2
                if mc < 2:
                    wo_dg = wo_pre[mc]
                    wov = wbuf[slot][:, 0:4096].rearrange("p (c w) -> p c w", w=128)
                else:
                    wo_dg, wov = load_w(slot, wo_d[mc], DC, 128)
                ps = psA if mc % 2 == 0 else psB
                p.wait_tok("tensor", wo_dg)
                p.wait("tensor", "dve", wo_ps_dve[mc % 2])
                idx = hp * 32 + mc
                page_load(idx, x1_d, mc, hp * 512)
                wp = 0
                for cc in range(DC):
                    wp = p.mm(ps[:, 0:512], wov[:, cc, :], YtC[:, cc, :],
                              start=(cc == 0), stop=(cc == DC - 1),
                              inc=(cc == DC - 1))
                p.wait("vector", "pe", wp)
                dst = x2_d if DEV_STOP != "p2" else y_d
                x2_last_store = page_evac_store(idx, ps[:, 0:512], dst, mc, hp * 512,
                                               bias_col=boc[:, mc:mc + 1])
                wo_ps_dve[mc % 2] = pg_dve[idx % 4]
                wslot_free[slot] = p.pe
                if hp == 1 and mc % 8 == 7:
                    x2_fen.append(p.fence_toks("sync"))
            att["last_exp"] = [0, 0, 0]
            att["eslot_pe"] = [0, 0, 0, 0]
            att["eslot_dve"] = [0, 0, 0, 0]
            att["psc_free"] = 0
            att["psob_free"] = 0
            p.wait("tensor", "act", p.act)
            p.wait("tensor", "dve", p.dve)
        return x2_last_store, x2_fen

    if DEV_STOP == "p1":
        plan_ffn(x_d, y_d, x_d, cached=False)
    else:
        st1, fen1, rbc1, mbc1, xbd1 = plan_ffn(x_d, x1_d, x_d, cached=False,
                                               fuse_ln1=True)
        cc1 = plan_ln1_gather(rbc1, mbc1, xbd1)
        st2, x2_fences = plan_attention(cc1)
        if DEV_STOP != "p2":
            p.join()
            plan_ffn(x2_d.rearrange("(c p) n -> p c n", p=128), y_d, x2_d,
                     cached=True, prev_fences=x2_fences)

    p.wait_tok("sync", p.fence_toks("sync"))
    p.wait_tok("gpsimd", p.fence_toks("gpsimd"))

    with nc.Block() as block:
        @block.tensor
        def _(E):
            for f in p.ops["tensor"]:
                f(E)

        @block.scalar
        def _(E):
            for f in p.ops["scalar"]:
                f(E)

        @block.vector
        def _(E):
            for f in p.ops["vector"]:
                f(E)

        @block.gpsimd
        def _(E):
            for f in p.ops["gpsimd"]:
                f(E)

        @block.sync
        def _(E):
            for f in p.ops["sync"]:
                f(E)

    es.close()
    return nc


_NC_CACHE = {}


def _get_nc():
    key = DEV_STOP
    if key not in _NC_CACHE:
        _NC_CACHE[key] = build_nc()
    return _NC_CACHE[key]


def kernel(L0_attn_hidden_state, L0_attn_Wq, L0_attn_bq, L0_attn_Wk, L0_attn_bk,
           L0_attn_Wv, L0_attn_bv, L0_attn_Wo, L0_attn_bo, L0_ffn_A, L0_ffn_B,
           L0_ffn_Wr):
    def pil(a, cb=128):
        R, C = a.shape
        return np.ascontiguousarray(
            a.reshape(R // 128, 128, C // cb, cb).transpose(2, 1, 0, 3))

    x = np.asarray(L0_attn_hidden_state, np.float32)[0].T
    Wq = np.asarray(L0_attn_Wq, np.float32)
    wr = pil(np.asarray(L0_ffn_Wr, np.float32).T, 64)[0]
    at = pil(np.asarray(L0_ffn_A, np.float32).reshape(1024, 4096).T)
    bt = pil(np.asarray(L0_ffn_B, np.float32).transpose(0, 2, 1).reshape(1024, 4096))
    wk = pil(np.asarray(L0_attn_Wk, np.float32).T)[0]
    wv = pil(np.asarray(L0_attn_Wv, np.float32).T)[0]
    wo = pil(np.asarray(L0_attn_Wo, np.float32).T)
    bq = np.asarray(L0_attn_bq, np.float32)
    bkc = np.ascontiguousarray(np.asarray(L0_attn_bk, np.float32).reshape(128, 1))
    bvc = np.ascontiguousarray(np.asarray(L0_attn_bv, np.float32).reshape(128, 1))
    boc = np.ascontiguousarray(np.asarray(L0_attn_bo, np.float32).reshape(32, 128).T)
    cvec = np.zeros((1, 8), np.float32)
    cvec[0, 0] = EPS
    cvec[0, 1] = float(np.log(S_ATTN))

    nc = _get_nc()
    in_maps = []
    for c in range(8):
        in_maps.append(dict(
            x=np.ascontiguousarray(
                x[:, c * TOK:(c + 1) * TOK].reshape(32, 128, TOK).transpose(1, 0, 2)),
            wr=wr, at=at, bt=bt,
            wq=pil(Wq[c * 512:(c + 1) * 512, :].T),
            wk=wk, wv=wv, wo=wo,
            bqc=np.ascontiguousarray(bq[c * 512:(c + 1) * 512].reshape(4, 128).T),
            bkc=bkc, bvc=bvc, boc=boc, cvec=cvec,
        ))
    res = run_bass_kernel_spmd(nc, in_maps, core_ids=list(range(8)))
    outs = [np.asarray(r["y"], np.float32).T for r in res.results]
    return np.concatenate(outs, axis=0)[None]


# version 20 (final)
# speedup vs baseline: 1.1531x; 1.1531x over previous
import numpy as np
import ml_dtypes
import concourse.bass as bass
import concourse.mybir as mybir
from concourse.bass_utils import run_bass_kernel_spmd
from contextlib import ExitStack

F32 = mybir.dt.float32
BF16 = mybir.dt.bfloat16
U8 = mybir.dt.uint8
AF = mybir.ActivationFunctionType
ALU = mybir.AluOpType
AX = mybir.AxisListType.X
BF = ml_dtypes.bfloat16

D = 4096
DC = 32
TOK = 1024
HD = 128
KR = 1024
S_ATTN = float(1.0 / np.sqrt(128.0))
EPS = 1e-5
INVD = float(1.0 / 4096.0)

DEV_STOP = None


class Pl:
    ENGS = ("tensor", "scalar", "vector", "gpsimd", "sync")

    NL = 4

    def __init__(self):
        self.ops = {e: [] for e in self.ENGS}
        self.pe = 0
        self.act = 0
        self.dve = 0
        self.gpn = 0
        self.cc = 0
        self.sems = {}
        self.rr = {"gpsimd": 0, "sync": 0}
        self.lane_cnt = {}
        for b in ("dg", "ds"):
            for l in range(self.NL):
                self.lane_cnt[b + str(l)] = 0

    def raw(self, eng, fn):
        self.ops[eng].append(fn)

    def wait(self, eng, sem, val):
        if val <= 0:
            return
        self.raw(eng, lambda E, s=sem, v=val: E.wait_ge(self.sems[s], v))

    def mm(self, out, lhsT, rhs, start=True, stop=True, inc=False):
        def f(E, o=out, l=lhsT, r=rhs, st=start, sp=stop, i=inc):
            ins = E.matmul(o, l, r, start=st, stop=sp)
            if i:
                ins.then_inc(self.sems["pe"], 1)
        self.raw("tensor", f)
        if inc:
            self.pe += 1
            return self.pe

    def tr(self, out, in_, ident, inc=False):
        def f(E, o=out, x=in_, idn=ident, i=inc):
            ins = E.transpose(o, x, idn)
            if i:
                ins.then_inc(self.sems["pe"], 1)
        self.raw("tensor", f)
        if inc:
            self.pe += 1
            return self.pe

    def sc(self, method, *a, inc=True, selfwait=False, **k):
        if selfwait:
            self.wait("scalar", "act", self.act)

        def f(E, m=method, aa=a, kk=k):
            getattr(E, m)(*aa, **kk).then_inc(self.sems["act"], 1)
        self.raw("scalar", f)
        self.act += 1
        return self.act

    def gp(self, method, *a, **k):
        self.wait("gpsimd", "gp", self.gpn)

        def f(E, m=method, aa=a, kk=k):
            getattr(E, m)(*aa, **kk).then_inc(self.sems["gp"], 1)
        self.raw("gpsimd", f)
        self.gpn += 1
        return self.gpn

    def ve(self, method, *a, **k):
        self.wait("vector", "dve", self.dve)

        def f(E, m=method, aa=a, kk=k):
            getattr(E, m)(*aa, **kk).then_inc(self.sems["dve"], 1)
        self.raw("vector", f)
        self.dve += 1
        return self.dve

    def dma(self, q, out, in_):
        base = "dg" if q == "gpsimd" else "ds"
        lane = base + str(self.rr[q] % self.NL)
        self.rr[q] += 1
        k = self.lane_cnt[lane]
        if k > 0:
            self.raw(q, lambda E, n=lane, v=16 * k: E.wait_ge(self.sems[n], v))

        def f(E, o=out, x=in_, n=lane):
            E.dma_start(out=o, in_=x).then_inc(self.sems[n], 16)
        self.raw(q, f)
        self.lane_cnt[lane] = k + 1
        return (lane, 16 * (k + 1))

    def wait_tok(self, eng, tok):
        if not tok:
            return
        if isinstance(tok, list):
            for t in tok:
                self.wait_tok(eng, t)
            return
        self.wait(eng, tok[0], tok[1])

    def fence_toks(self, q):
        base = "dg" if q == "gpsimd" else "ds"
        return [(base + str(l), 16 * self.lane_cnt[base + str(l)])
                for l in range(self.NL) if self.lane_cnt[base + str(l)] > 0]

    @staticmethod
    def tok_max(a, b):
        if not a:
            return b
        if not b:
            return a
        if isinstance(a, list) or isinstance(b, list):
            aa = a if isinstance(a, list) else [a]
            bb = b if isinstance(b, list) else [b]
            return aa + bb
        if a[0] == b[0]:
            return a if a[1] >= b[1] else b
        return [a, b]

    def join(self):
        pe, act, dve = self.pe, self.act, self.dve
        self.wait("tensor", "act", act)
        self.wait("tensor", "dve", dve)
        self.wait("scalar", "pe", pe)
        self.wait("scalar", "dve", dve)
        self.wait("vector", "pe", pe)
        self.wait("vector", "act", act)


def build_nc():
    nc = bass.Bass("TRN2", target_bir_lowering=False)

    x_d = nc.dram_tensor("x", [128, DC, TOK], F32, kind="ExternalInput").ap()
    wr_d = nc.dram_tensor("wr", [128, DC, 64], F32, kind="ExternalInput").ap()
    at_d = nc.dram_tensor("at", [8, 128, DC, 128], F32, kind="ExternalInput").ap()
    bt_d = nc.dram_tensor("bt", [DC, 128, 8, 128], F32, kind="ExternalInput").ap()
    wq_d = nc.dram_tensor("wq", [4, 128, DC, 128], F32, kind="ExternalInput").ap()
    wk_d = nc.dram_tensor("wk", [128, DC, 128], F32, kind="ExternalInput").ap()
    wv_d = nc.dram_tensor("wv", [128, DC, 128], F32, kind="ExternalInput").ap()
    wo_d = nc.dram_tensor("wo", [DC, 128, DC, 128], F32, kind="ExternalInput").ap()
    bqc_d = nc.dram_tensor("bqc", [128, 4], F32, kind="ExternalInput").ap()
    bkc_d = nc.dram_tensor("bkc", [128, 1], F32, kind="ExternalInput").ap()
    bvc_d = nc.dram_tensor("bvc", [128, 1], F32, kind="ExternalInput").ap()
    boc_d = nc.dram_tensor("boc", [128, 32], F32, kind="ExternalInput").ap()
    cvec_d = nc.dram_tensor("cvec", [1, 8], F32, kind="ExternalInput").ap()
    y_d = nc.dram_tensor("y", [D, TOK], F32, kind="ExternalOutput").ap()
    dbg_d = (nc.dram_tensor("dbg", [128, 32768], U8, kind="ExternalOutput").ap()
             if DEV_STOP == "p2" else None)

    x1_d = nc.dram_tensor("x1buf", [D, TOK], F32).ap()
    a16_d = nc.dram_tensor("a16buf", [D, KR], BF16).ap()
    b16_d = nc.dram_tensor("b16buf", [KR, D], BF16).ap()
    wo16_d = nc.dram_tensor("wo16buf", [D, D], BF16).ap()
    x2_d = nc.dram_tensor("x2buf", [D, TOK], F32).ap()
    gin_d = nc.dram_tensor("ginbuf", [D, TOK], BF16).ap()
    gouts = [nc.dram_tensor(f"goutbuf{g}", [8 * 1024, TOK], BF16,
                            addr_space="Shared").ap() for g in range(4)]
    gin2_d = nc.dram_tensor("gin2buf", [2 * HD, TOK], BF16).ap()
    gout2_d = nc.dram_tensor("gout2buf", [8 * 2 * HD, TOK], BF16,
                             addr_space="Shared").ap()

    id16_h = nc.inline_tensor(np.eye(128, dtype=BF), "id16c")
    onescol_h = nc.inline_tensor(np.ones((128, 1), dtype=BF), "onescolc")
    onesrow_h = nc.inline_tensor(np.ones((1, 128), dtype=BF), "onesrowc")
    expand_np = np.zeros((64, 1024), dtype=BF)
    for krx in range(1024):
        expand_np[krx // 16, krx] = 1.0
    expand_h = nc.inline_tensor(expand_np, "expandc")

    p = Pl()
    es = ExitStack()

    def sbt(name, shape, dtype):
        return es.enter_context(nc.sbuf_tensor(name, shape, dtype))

    xbu = sbt("xbu", [128, 65536], U8)
    ybu = sbt("ybu", [128, 32768], U8)
    ar1 = sbt("ar1", [128, 16384], U8)
    ar2 = sbt("ar2", [128, 16384], U8)
    ar3 = sbt("ar3", [128, 8192], U8)
    arM = sbt("arM", [128, 4096], U8)

    def carve(t, byte_off, dtype, shape):
        nb = int(np.prod(shape[1:])) * mybir.dt.size(dtype)
        ap = t[:, byte_off:byte_off + nb].bitcast(dtype)
        if len(shape) > 2:
            kw = {f"d{i}": shape[i] for i in range(1, len(shape))}
            dims = " ".join(f"d{i}" for i in range(1, len(shape)))
            ap = ap.rearrange(f"p ({dims}) -> p {dims}", **kw)
        if shape[0] != 128:
            ap = ap[0:shape[0]]
        return ap

    xb = carve(xbu, 0, BF16, [128, DC, TOK])
    Qv = carve(xbu, 0, BF16, [128, 2, 8192])
    lnring = [carve(xbu, 32768 + j * 16384, BF16, [128, DC, 256]) for j in range(2)]
    Yt = carve(ybu, 0, BF16, [128, 32, 2, 128, 2])
    YtC = carve(ybu, 0, BF16, [128, 32, 512])
    Vfm = carve(ybu, 0, BF16, [128, 8192])
    Gexp = carve(ar1, 0, BF16, [128, 8, TOK])
    Ksb = carve(ar1, 0, BF16, [128, 8192])
    hG = carve(ar2, 0, BF16, [128, 8, TOK])
    Vtm = carve(ar2, 0, BF16, [128, 8192])
    Ering = carve(ar3, 0, BF16, [128, 4, 1024])
    O16 = carve(arM, 0, BF16, [128, 512])
    qsq16 = carve(arM, 1024, BF16, [128, 512])
    mrg16 = carve(arM, 2048, BF16, [128, 1024])
    gmx8 = sbt("gmx8", [128, 64], F32)
    O16b = carve(arM, 1024, BF16, [128, 512])
    sqA = carve(arM, 0, BF16, [128, 1024])
    sqB = carve(arM, 2048, BF16, [128, 1024])
    sc_tm = carve(arM, 0, F32, [128, 512])
    sc_zap = carve(arM, 2048, F32, [128, 512])
    m_bc = carve(arM, 0, F32, [128, TOK])

    r_bc = sbt("r_bc", [128, TOK], F32)[:, :]
    wbuf = [sbt("wbuf0", [128, 4096], BF16), sbt("wbuf1", [128, 4096], BF16)]
    xpg = sbt("xpg", [128, 4, 512], F32)[:, :, :]
    xout = sbt("xout", [128, 4, 512], F32)[:, :, :]
    xsq = sbt("xsqr", [128, 2, TOK], BF16)[:, :, :]
    scr = sbt("scrf", [128, TOK], F32)[:, :]
    rows16 = sbt("rows16", [128, TOK], BF16)[:, :]
    rows5f = sbt("rows5f", [128, 512], F32)[:, :]
    rows5h = sbt("rows5h", [128, 512], BF16)[:, :]
    tinyf = sbt("tinyf", [128, 16], F32)[:, :]
    tiny16 = sbt("tiny16", [128, 8], BF16)[:, :]
    m8a = sbt("m8a", [128, 8], F32)[:, :]
    m8b = sbt("m8b", [128, 8], F32)[:, :]
    gneg = sbt("gneg", [128, 1], F32)[:, :]
    gsumr = sbt("gsumr", [128, 2], F32)[:, :]
    Gfm16 = sbt("Gfm16", [64, TOK], BF16)[0:64, :]
    sumrow = sbt("sumrow", [1, KR], BF16)
    rhis = sbt("rhis", [1, TOK], BF16)[0:1, :]
    rlos = sbt("rlos", [1, TOK], BF16)[0:1, :]
    lo5 = sbt("lo5", [1, 512], BF16)[0:1, :]
    wrsum = sbt("wrsum", [1, 64], BF16)
    negC2h = sbt("negC2h", [128, 2], F32)
    id16 = sbt("id16s", [128, 128], BF16)
    ones_c = sbt("ones_cs", [128, 1], BF16)
    ones_r = sbt("ones_rs", [1, 128], BF16)
    bqc = sbt("bqc_s", [128, 4], F32)
    bkc = sbt("bkc_s", [128, 1], F32)
    bvc = sbt("bvc_s", [128, 1], F32)
    boc = sbt("boc_s", [128, 32], F32)
    cvec = sbt("cvec_s", [1, 8], F32)

    psA = es.enter_context(nc.psum_tensor("psA", [128, 1024], F32))[:, :]
    psB = es.enter_context(nc.psum_tensor("psB", [128, 1024], F32))[:, :]
    psDE = es.enter_context(nc.psum_tensor("psDE", [128, 1024], F32))[:, :]
    psC = es.enter_context(nc.psum_tensor("psC", [128, 512], F32))[:, :]
    psT = es.enter_context(nc.psum_tensor("psT", [128, 1024], BF16))[:, :]
    psD = psDE[:, 0:512]
    psE = psDE[:, 512:1024]

    semnames = ["pe", "act", "dve", "cc", "gp"]
    for b in ("dg", "ds"):
        for l in range(Pl.NL):
            semnames.append(b + str(l))
    for sname in semnames:
        p.sems[sname] = es.enter_context(nc.semaphore("sem_" + sname))

    ctoks = []
    for dst, src in ((id16[:, :], id16_h.ap()), (ones_c[:, :], onescol_h.ap()),
                     (ones_r[:, :], onesrow_h.ap())):
        ctoks.append(p.dma("gpsimd", dst, src))
    for dst, src in ((bqc[:, :], bqc_d), (bkc[:, :], bkc_d), (bvc[:, :], bvc_d),
                     (boc[:, :], boc_d), (cvec[:, :], cvec_d)):
        ctoks.append(p.dma("sync", dst, src))
    for e in ("tensor", "scalar", "vector"):
        p.wait_tok(e, ctoks)

    accP = [xpg[:, 0:2, :].rearrange("p a b -> p (a b)"),
            xpg[:, 2:4, :].rearrange("p a b -> p (a b)")]
    emask = sbt("emask", [128, 512], BF16)[:, :]
    Gtm16 = rows5h
    expand16 = rows5f.bitcast(BF16)[0:64, :]
    st_m = carve(ar2, 0, F32, [1, TOK])
    st_ss = carve(ar2, 4096, F32, [1, TOK])
    st_msq = carve(ar2, 8192, F32, [1, TOK])
    st_var = carve(ar2, 12288, F32, [1, TOK])
    st_lnv = carve(ar1, 0, F32, [1, TOK])
    st_rstd = carve(ar1, 4096, F32, [1, TOK])
    scfm = carve(ar2, 0, F32, [64, TOK])
    scr64 = scr[0:64, :]
    schi = xsq[0:64, 0, :]
    sclo = xsq[0:64, 1, :]
    kvloc = xsq[:, :, :]

    pg_store = [None, None, None, None]
    pg_load = [None, None, None, None]
    pg_dve = [0, 0, 0, 0]

    def page_load(idx, src_dram, dc, col0):
        sl = idx % 4
        p.wait("sync", "dve", pg_dve[sl])
        if src_dram is x_d:
            srcap = x_d[:, dc, col0:col0 + 512]
        else:
            srcap = src_dram[dc * 128:(dc + 1) * 128, col0:col0 + 512]
        tok = p.dma("sync", xpg[:, sl, :], srcap)
        pg_load[sl] = tok
        return tok

    def page_evac_store(idx, ps_ap, dst_dram, dc, col0, bias_col=None):
        sl = idx % 4
        p.wait_tok("vector", Pl.tok_max(pg_load[sl], pg_store[sl]))
        if bias_col is None:
            n = p.ve("tensor_tensor", xout[:, sl, :], ps_ap, xpg[:, sl, :], ALU.add)
        else:
            n = p.ve("scalar_tensor_tensor", xout[:, sl, :], ps_ap, bias_col,
                     xpg[:, sl, :], ALU.add, ALU.add)
        pg_dve[sl] = n
        p.wait("sync", "dve", n)
        tok = p.dma("sync", dst_dram[dc * 128:(dc + 1) * 128, col0:col0 + 512],
                    xout[:, sl, :])
        pg_store[sl] = tok
        return tok

    wslot_free = [0, 0]
    wslot_wb = [None, None]

    def load_w(slot, dram_view, nchunks, width, queue="gpsimd", wb_view=None):
        p.wait(queue, "pe", wslot_free[slot])
        p.wait_tok(queue, wslot_wb[slot])
        wslot_wb[slot] = None
        view = wbuf[slot][:, 0:nchunks * width].rearrange("p (c w) -> p c w", w=width)
        tok = p.dma(queue, view, dram_view)
        if wb_view is not None:
            p.wait_tok("sync", tok)
            wslot_wb[slot] = p.dma("sync", wb_view, view)
        return tok, view

    def ln_rows(sxA, sxB, sq, want_mbc):
        p.sc("activation", st_m[:, 0:512], sxA, AF.Copy, scale=INVD)
        p.sc("activation", st_m[:, 512:1024], sxB, AF.Copy, scale=INVD)
        am = p.sc("activation", st_ss[:, :], sq, AF.Copy, scale=INVD)
        a1 = p.sc("activation", st_msq[:, :], st_m[:, :], AF.Square,
                  selfwait=True)
        p.wait("vector", "act", a1)
        nvar = p.ve("tensor_tensor", st_var[:, :], st_ss[:, :], st_msq[:, :],
                    ALU.subtract)
        p.wait("scalar", "dve", nvar)
        p.sc("activation", st_lnv[:, :], st_var[:, :], AF.Ln, bias=cvec[0:1, 0:1])
        a2 = p.sc("activation", st_rstd[:, :], st_lnv[:, :], AF.Exp, scale=-0.5,
                  selfwait=True)
        p.ve("tensor_scalar", rows16[0:1, :], st_m[:, :], -1.0, None, ALU.mult)
        p.wait("vector", "act", a2)
        p.ve("tensor_copy", rhis, st_rstd[:, :])
        p.ve("tensor_tensor", scr[0:1, :], st_rstd[:, :], rhis, ALU.subtract)
        nlo = p.ve("tensor_copy", rlos, scr[0:1, :])
        p.wait("tensor", "dve", nlo)
        p.wait("tensor", "act", am)
        rbc_act = 0
        for nch in range(2):
            cs = slice(nch * 512, (nch + 1) * 512)
            p.mm(psB[:, cs], ones_r[:, :], rhis[:, cs], start=True, stop=False)
            bcp = p.mm(psB[:, cs], ones_r[:, :], rlos[:, cs], start=False,
                       stop=True, inc=True)
            p.wait("scalar", "pe", bcp)
            rbc_act = p.sc("copy", r_bc[:, cs], psB[:, cs])
        mbc_act = 0
        if want_mbc:
            p.wait("tensor", "act", rbc_act)
            p.wait("vector", "pe", p.pe)
            for nch in range(2):
                cs = slice(nch * 512, (nch + 1) * 512)
                p.wait("vector", "pe", p.pe)
                p.ve("tensor_copy", rhis[:, 0:512], st_m[:, cs])
                p.ve("tensor_tensor", scr[0:1, 0:512], st_m[:, cs],
                     rhis[:, 0:512], ALU.subtract)
                nl = p.ve("tensor_copy", lo5, scr[0:1, 0:512])
                p.wait("tensor", "dve", nl)
                if nch == 1:
                    p.wait("tensor", "act", mbc_act)
                p.mm(psB[:, cs], ones_r[:, :], rhis[:, 0:512], start=True, stop=False)
                bcp = p.mm(psB[:, cs], ones_r[:, :], lo5, start=False,
                           stop=True, inc=True)
                p.wait("scalar", "pe", bcp)
                mbc_act = p.sc("copy", m_bc[:, cs], psB[:, cs])
        return rbc_act, mbc_act

    def ln_stats(xb_dg, want_mbc):
        if not isinstance(xb_dg, list):
            xb_dg = [xb_dg]
        last = 0
        slot_pe = [0, 0]
        for dc in range(DC):
            rg = dc % 2
            gi = min(dc // 8, len(xb_dg) - 1)
            p.wait_tok("vector", xb_dg[gi])
            p.wait_tok("tensor", xb_dg[gi])
            p.wait("vector", "pe", slot_pe[rg])
            nd = p.ve("tensor_tensor", xsq[:, rg, :], xb[:, dc, :], xb[:, dc, :],
                      ALU.mult)
            p.mm(psC[0:1, :], ones_c[:, :], xb[:, dc, 0:512],
                 start=(dc == 0), stop=(dc == DC - 1))
            p.mm(psD[0:1, :], ones_c[:, :], xb[:, dc, 512:1024],
                 start=(dc == 0), stop=(dc == DC - 1))
            p.wait("tensor", "dve", nd)
            p.mm(psA[0:1, 0:512], ones_c[:, :], xsq[:, rg, 0:512],
                 start=(dc == 0), stop=(dc == DC - 1))
            last = p.mm(psA[0:1, 512:1024], ones_c[:, :], xsq[:, rg, 512:1024],
                        start=(dc == 0), stop=(dc == DC - 1), inc=True)
            slot_pe[rg] = last
        p.wait("scalar", "pe", last)
        return ln_rows(psC[0:1, :], psD[0:1, :], psA[0:1, :], want_mbc)

    def plan_ffn(src_view, dst_dram, resid_dram, cached, prev_fences=None,
                 fuse_ln1=False):
        p.join()
        p.wait("gpsimd", "pe", p.pe)
        p.wait("gpsimd", "act", p.act)
        p.wait("gpsimd", "dve", p.dve)
        xb_dg = []
        for g in range(4):
            if prev_fences is not None:
                p.wait_tok("gpsimd", prev_fences[g])
            xb_dg.append(p.dma("gpsimd", xb[:, g * 8:(g + 1) * 8, :],
                               src_view[:, g * 8:(g + 1) * 8, :]))
        wr_dg, wrv = load_w(0, wr_d, DC, 64)
        exp_dg = p.dma("gpsimd", expand16, expand_h.ap()[0:64, :])
        rbc_act, _ = ln_stats(xb_dg, want_mbc=False)

        def a_src(kc):
            return (at_d[kc], "gpsimd", None)

        def b_src(dc):
            return (bt_d[dc], "gpsimd", None)

        p.wait_tok("tensor", xb_dg[-1] if isinstance(xb_dg, list) else xb_dg)
        p.wait_tok("tensor", wr_dg)
        wr_pe = 0
        for dc in range(DC):
            wr_pe = p.mm(psE[0:1, 0:64], ones_c[:, :], wrv[:, dc, :],
                         start=(dc == 0), stop=(dc == DC - 1), inc=(dc == DC - 1))
        p.wait("scalar", "pe", wr_pe)
        awr = p.sc("copy", wrsum[0:1, :], psE[0:1, 0:64], inc=True)
        p.wait("tensor", "act", awr)
        p.wait("tensor", "act", rbc_act)
        p.wait("vector", "act", rbc_act)
        router_dve = 0
        for nch in range(2):
            cs = slice(nch * 512, (nch + 1) * 512)
            for dc in range(DC):
                p.mm(psB[0:64, cs], wrv[:, dc, :], xb[:, dc, cs],
                     start=(dc == 0), stop=False)
            rt = p.mm(psB[0:64, cs], wrsum[0:1, :], rows16[0:1, cs],
                      start=False, stop=True, inc=True)
            p.wait("vector", "pe", rt)
            router_dve = p.ve("tensor_tensor", scfm[:, cs], psB[0:64, cs],
                              r_bc[0:64, cs], ALU.mult)
        wslot_free[0] = p.pe

        hg_dve = [0] * 8
        hpe = {}
        ffn_state = dict(ak_prev=0)

        def h_mms(kc):
            dview, q, wb = a_src(kc)
            slot = kc % 2
            a_dg, av = load_w(slot, dview, DC, 128, queue=q, wb_view=wb)
            ps = psA if kc % 2 == 0 else psB
            p.wait_tok("tensor", a_dg)
            if kc >= 2:
                p.wait("tensor", "dve", hg_dve[kc - 2])
            elif kc == 1:
                p.wait("tensor", "dve", router_dve)
            p.wait("tensor", "act", ffn_state["ak_prev"])
            ap_ = 0
            for dc in range(DC):
                ap_ = p.mm(psE[0:1, 0:128], ones_c[:, :], av[:, dc, :],
                           start=(dc == 0), stop=(dc == DC - 1), inc=(dc == DC - 1))
            p.wait("scalar", "pe", ap_)
            aas = p.sc("copy", sumrow[0:1, kc * 128:(kc + 1) * 128],
                       psE[0:1, 0:128])
            ffn_state["ak_prev"] = aas
            p.wait("tensor", "act", aas)
            for nch in range(2):
                cs = slice(nch * 512, (nch + 1) * 512)
                for dc in range(DC):
                    p.mm(ps[:, cs], av[:, dc, :], xb[:, dc, cs],
                         start=(dc == 0), stop=False)
                hpe[(kc, nch)] = p.mm(ps[:, cs],
                                      sumrow[0:1, kc * 128:(kc + 1) * 128],
                                      rows16[0:1, cs], start=False, stop=True,
                                      inc=True)
            wslot_free[slot] = p.pe

        def hg_evac(kc, ge_act):
            ps = psA if kc % 2 == 0 else psB
            for nch in range(2):
                cs = slice(nch * 512, (nch + 1) * 512)
                p.wait("vector", "pe", hpe[(kc, nch)])
                p.wait("vector", "act", ge_act[kc * 2 + nch])
                hg_dve[kc] = p.ve("tensor_tensor", hG[:, kc, cs], ps[:, cs],
                                  Gexp[:, kc, cs], ALU.mult)

        h_mms(0)
        h_mms(1)

        p.ve("tensor_copy", schi[:, :], scfm[:, :])
        p.ve("tensor_tensor", scr64[:, :], scfm[:, :], schi[:, :], ALU.subtract)
        nlo = p.ve("tensor_copy", sclo[:, :], scr64[:, :])
        p.wait("tensor", "dve", nlo)
        tpe = 0
        for j in range(8):
            p.tr(psT[:, j * 64:(j + 1) * 64], schi[:, j * 128:(j + 1) * 128],
                 id16[0:64, 0:64])
        for j in range(8):
            tpe = p.tr(psT[:, 512 + j * 64:512 + (j + 1) * 64],
                       sclo[:, j * 128:(j + 1) * 128], id16[0:64, 0:64], inc=(j == 7))
        p.wait("scalar", "pe", tpe)
        ahi = p.sc("copy", sc_tm[:, :], psT[:, 0:512])
        p.wait("vector", "act", ahi)
        p.ve("tensor_tensor", sc_tm[:, :], sc_tm[:, :], psT[:, 512:1024], ALU.add)
        for j in range(8):
            t = slice(j * 64, (j + 1) * 64)
            p.ve("max", gmx8[:, j * 8:(j + 1) * 8], sc_tm[:, t])
            p.ve("match_replace", out=sc_zap[:, t],
                 in_to_replace=gmx8[:, j * 8:(j + 1) * 8],
                 in_values=sc_tm[:, t], imm_value=-1e30)
            p.ve("max", m8b[:, :], sc_zap[:, t])
            p.ve("match_replace", out=scr[:, 512 + j * 64:512 + (j + 1) * 64],
                 in_to_replace=m8b[:, :],
                 in_values=sc_zap[:, t], imm_value=-1e30)
        gseg = lambda ap: ap.rearrange("p (s o) -> p s o", s=8)
        nsub = p.ve("tensor_tensor", gseg(scr[:, 0:512]), gseg(sc_tm[:, :]),
                    gmx8[:, :].rearrange("p (s o) -> p s o", o=8)[:, :, 0:1]
                    .broadcast_to([128, 8, 64]), ALU.subtract)
        p.wait("scalar", "dve", nsub)
        ag = p.sc("activation", sc_zap[:, :], scr[:, 0:512], AF.Exp)
        p.ve("tensor_scalar", emask[:, :], scr[:, 512:1024], -1e29, None,
             ALU.is_le)
        p.wait("vector", "act", ag)
        p.ve("tensor_tensor", sc_tm[:, :], sc_zap[:, :], emask[:, :], ALU.mult)
        p.ve("reduce_sum", m8b[:, :].rearrange("p (s o) -> p s o", o=1),
             gseg(sc_tm[:, :]), axis=AX)
        p.ve("reciprocal", m8a[:, :], m8b[:, :])
        ng = p.ve("tensor_tensor", gseg(Gtm16[:, :]), gseg(sc_tm[:, :]),
                  m8a[:, :].rearrange("p (s o) -> p s o", o=1)
                  .broadcast_to([128, 8, 64]), ALU.mult)
        p.wait("tensor", "dve", ng)
        gt = 0
        for j in range(8):
            gt = p.tr(psT[0:64, j * 128:(j + 1) * 128], Gtm16[:, j * 64:(j + 1) * 64],
                      id16[:, :], inc=(j == 7))
        p.wait("vector", "pe", gt)
        ngf = 0
        for j in range(8):
            ngf = p.ve("tensor_tensor", Gfm16[:, j * 128:(j + 1) * 128],
                       psT[0:64, j * 128:(j + 1) * 128],
                       r_bc[0:64, j * 128:(j + 1) * 128], ALU.mult)
        p.wait("tensor", "dve", ngf)
        p.wait_tok("tensor", exp_dg)
        ge_act = []
        for kc in range(8):
            for nch in range(2):
                ps = psC if (kc * 2 + nch) % 2 == 0 else psD
                if len(ge_act) >= 2:
                    p.wait("tensor", "act", ge_act[-2])
                ge = p.mm(ps[:, 0:512], expand16[:, kc * 128:(kc + 1) * 128],
                          Gfm16[:, nch * 512:(nch + 1) * 512], inc=True)
                p.wait("scalar", "pe", ge)
                ge_act.append(p.sc("copy", Gexp[:, kc, nch * 512:(nch + 1) * 512],
                                   ps[:, 0:512]))
        hg_evac(0, ge_act)
        hg_evac(1, ge_act)
        for kc in range(2, 8):
            h_mms(kc)
            hg_evac(kc, ge_act)
        p.wait("tensor", "act", p.act)
        out_ps_dve = [0, 0]
        last_store = 0
        bsub_free = [wslot_free[0], wslot_free[1]] * 4
        bsub_tok = [None] * 8

        def bload(dc):
            sub = dc % 8
            view = wbuf[sub % 2][:, (sub // 2) * 1024:(sub // 2 + 1) * 1024]
            view = view.rearrange("p (c w) -> p c w", w=128)
            p.wait("gpsimd", "pe", bsub_free[sub])
            bsub_tok[sub] = p.dma("gpsimd", view, bt_d[dc])
            return view

        bviews = {}
        for dc in range(min(8, DC)):
            bviews[dc] = bload(dc)
        oct_fences = []
        pg_cast = [0, 0, 0, 0]
        xsq_pe = [0, 0]
        xbcast_act = 0
        stat_pe = 0
        for dc in range(DC):
            sub = dc % 8
            bv = bviews.pop(dc)
            ps = psC if dc % 2 == 0 else psD
            p.wait_tok("tensor", bsub_tok[sub])
            if dc == 0:
                p.wait("tensor", "dve", hg_dve[7])
            for nch in range(2):
                idx = dc * 2 + nch
                page_load(idx, resid_dram, dc, nch * 512)
                p.wait("tensor", "dve", out_ps_dve[dc % 2])
                op = 0
                for kc in range(8):
                    op = p.mm(ps[:, 0:512], bv[:, kc, :],
                              hG[:, kc, nch * 512:(nch + 1) * 512],
                              start=(kc == 0), stop=(kc == 7), inc=(kc == 7))
                p.wait("vector", "pe", op)
                sl = idx % 4
                p.wait("vector", "act", pg_cast[sl])
                last_store = page_evac_store(idx, ps[:, 0:512], dst_dram, dc,
                                             nch * 512)
                out_ps_dve[dc % 2] = pg_dve[idx % 4]
                if fuse_ln1:
                    cs = slice(nch * 512, (nch + 1) * 512)
                    p.wait("scalar", "dve", pg_dve[sl])
                    xbcast_act = p.sc("copy", xb[:, dc, cs], xout[:, sl, :])
                    pg_cast[sl] = xbcast_act
                    p.wait("vector", "act", xbcast_act)
                    p.wait("vector", "pe", xsq_pe[idx % 2])
                    nq = p.ve("tensor_tensor", xsq[:, idx % 2, 0:512],
                              xb[:, dc, cs], xb[:, dc, cs], ALU.mult)
                    p.wait("tensor", "dve", nq)
                    p.mm(psA[0:1, cs], ones_c[:, :], xb[:, dc, cs],
                         start=(dc == 0), stop=(dc == DC - 1))
                    stat_pe = p.mm(psB[0:1, cs], ones_c[:, :],
                                   xsq[:, idx % 2, 0:512],
                                   start=(dc == 0), stop=(dc == DC - 1),
                                   inc=True)
                    xsq_pe[idx % 2] = stat_pe
            bsub_free[sub] = p.pe
            if dc + 8 < DC:
                bviews[dc + 8] = bload(dc + 8)
            if dc % 8 == 7:
                oct_fences.append(p.fence_toks("sync"))
        wslot_free[0] = p.pe
        wslot_free[1] = p.pe
        if fuse_ln1:
            p.wait("scalar", "pe", stat_pe)
            rbc_act, mbc_act = ln_rows(psA[0:1, 0:512], psA[0:1, 512:1024],
                                       psB[0:1, :], want_mbc=True)
            return last_store, oct_fences, rbc_act, mbc_act, xbcast_act
        return last_store, oct_fences

    def plan_ln1_gather(rbc_act, mbc_act, xb_done_act):
        p.join()
        p.wait("gpsimd", "pe", p.pe)
        p.wait("vector", "act", max(rbc_act, mbc_act, xb_done_act))
        p.wait("vector", "pe", p.pe)
        apply_dve = [0] * DC
        wk_dg, wkv = load_w(0, wk_d, DC, 128)
        wv_dg, wvv = load_w(1, wv_d, DC, 128)
        for dc in range(DC):
            p.ve("tensor_tensor", scr[:, :], xb[:, dc, :], m_bc[:, :], ALU.subtract)
            apply_dve[dc] = p.ve("tensor_tensor", xb[:, dc, :], scr[:, :], r_bc[:, :],
                                 ALU.mult)
        for g in range(4):
            p.wait("sync", "dve", apply_dve[g * 8 + 7])
            p.dma("sync",
                  gin_d.rearrange("(c p) n -> p c n", p=128)[:, g * 8:(g + 1) * 8, :],
                  xb[:, g * 8:(g + 1) * 8, :])
            p.wait_tok("gpsimd", p.fence_toks("sync"))

            def collg(E, gg=g):
                E.collective_compute(
                    "AllGather", ALU.bypass,
                    replica_groups=[list(range(8))],
                    ins=[gin_d[gg * 1024:(gg + 1) * 1024, :].opt()],
                    outs=[gouts[gg].opt()],
                ).then_inc(p.sems["cc"], 1)
            p.raw("gpsimd", collg)
            p.cc += 1
        kv_act = 0
        p.wait_tok("tensor", wk_dg)
        p.wait_tok("tensor", wv_dg)
        for which in range(2):
            wvw = wkv if which == 0 else wvv
            bias = bkc if which == 0 else bvc
            ps = psC if which == 0 else psD
            for nch in range(2):
                cs = slice(nch * 512, (nch + 1) * 512)
                if which == 1 or nch == 1:
                    p.wait("tensor", "act", kv_act)
                kp = 0
                for dc in range(DC):
                    p.wait("tensor", "dve", apply_dve[dc])
                    kp = p.mm(ps[:, 0:512], wvw[:, dc, :], xb[:, dc, cs],
                              start=(dc == 0), stop=(dc == DC - 1),
                              inc=(dc == DC - 1))
                p.wait("scalar", "pe", kp)
                kv_act = p.sc("activation", kvloc[:, which, cs], ps[:, 0:512],
                              AF.Identity, bias=bias[:, 0:1])
        wslot_free[0] = p.pe
        wslot_free[1] = p.pe
        p.wait("sync", "act", kv_act)
        p.dma("sync", gin2_d[0:HD, :], kvloc[:, 0, :])
        p.dma("sync", gin2_d[HD:2 * HD, :], kvloc[:, 1, :])
        p.wait_tok("gpsimd", p.fence_toks("sync"))

        def coll2(E):
            E.collective_compute(
                "AllGather", ALU.bypass,
                replica_groups=[list(range(8))],
                ins=[gin2_d.opt()],
                outs=[gout2_d.opt()],
            ).then_inc(p.sems["cc"], 1)
        p.raw("gpsimd", coll2)
        p.cc += 1
        return p.cc

    att = dict(last_exp=[0, 0, 0], eslot_pe=[0, 0, 0, 0], eslot_dve=[0, 0, 0, 0],
               psc_free=0, psob_free=0)

    def _av2(kt, exp_of_kt, psOb):
        p.wait("tensor", "act", exp_of_kt)
        if kt == 0:
            p.wait("tensor", "act", att["psc_free"])
            p.wait("tensor", "dve", att["psob_free"])
        p.mm(psC[:, :], Vtm[:, kt * 128:(kt + 1) * 128],
             Ering[:, kt % 4, 0:512],
             start=(kt == 0), stop=(kt == 63))
        cnt = p.mm(psOb, Vtm[:, kt * 128:(kt + 1) * 128],
                   Ering[:, kt % 4, 512:1024],
                   start=(kt == 0), stop=(kt == 63), inc=True)
        att["eslot_pe"][kt % 4] = cnt
        return cnt

    def plan_attention(cc_ready):
        p.join()
        p.wait("sync", "pe", p.pe)
        p.wait("sync", "act", p.act)
        p.wait("sync", "dve", p.dve)
        km_dve = 0
        scr_row = scr[0:1, 0:1024].rearrange("p (s o) -> p s o", s=2)
        seg_out = tinyf[0:1, 12:14].rearrange("p (s o) -> p s o", o=1)

        def prep_mms(j, st):
            p.wait("tensor", "dve", st["dv"][j])
            p.wait("tensor", "act", st["act_entry"])
            p.wait("tensor", "act", st["cp"])
            sq = (sqA, sqB)[j % 2]
            p.mm(psD[0:1, 0:512], ones_c[:, :], sq[:, 0:512],
                 start=True, stop=True)
            kp = p.mm(psE[0:1, 0:512], ones_c[:, :], sq[:, 512:1024],
                      start=True, stop=True, inc=True)
            st["kp"][j % 2] = kp
            p.wait("scalar", "pe", kp)
            if j == 0:
                p.wait("scalar", "dve", st["dve_entry"])
            p.wait("scalar", "dve", st["rm"])
            p.sc("copy", scr[0:1, 0:512], psD[0:1, 0:512])
            st["cp"] = p.sc("copy", scr[0:1, 512:1024], psE[0:1, 0:512],
                            inc=True)
            p.wait("vector", "act", st["cp"])
            p.ve("reduce_max", seg_out, scr_row, axis=AX)
            if j == 0:
                st["rm"] = p.ve("tensor_copy", tinyf[0:1, 6:8],
                                tinyf[0:1, 12:14])
            else:
                st["rm"] = p.ve("tensor_tensor", tinyf[0:1, 6:8],
                                tinyf[0:1, 6:8], tinyf[0:1, 12:14], ALU.max)
        ps_ring = (psA, psB, psDE)
        x2_last_store = 0
        x2_fen = []
        for hp in range(2):
            wdg = [None, None]
            for oc in range(2):
                wdg[oc], _ = load_w(oc, wq_d[hp * 2 + oc], DC, 128)
            lnld = [0, 0]
            q_act = 0
            qp_act4 = [0, 0, 0, 0]
            st = dict(dv=[0] * 16, kp=[0, 0], cp=0, rm=0,
                      act_entry=p.act, dve_entry=p.dve)
            p.wait("scalar", "pe", p.pe)
            for qc in range(32):
                rg = qc % 2
                rr = (qc * 256) // 1024
                c0 = (qc * 256) % 1024
                p.wait("sync", "pe", lnld[rg])
                lcs = []
                for g in range(4):
                    p.wait("sync", "cc", g + 1)
                    lcs.append(p.dma(
                        "sync", lnring[rg][:, g * 8:(g + 1) * 8, :],
                        gouts[g][rr * 1024:(rr + 1) * 1024,
                                 c0:c0 + 256].rearrange(
                            "(c p) n -> p c n", p=128)))
                p.wait_tok("tensor", wdg)
                for oc in range(2):
                    pslot = (qc * 2 + oc) % 4
                    psq = (psA, psA, psB, psB)[pslot]
                    coff = (0, 512, 0, 512)[pslot]
                    p.wait("tensor", "act", qp_act4[pslot])
                    qp = 0
                    for dc in range(DC):
                        if dc % 8 == 0 and oc == 0:
                            p.wait_tok("tensor", lcs[dc // 8])
                        qp = p.mm(psq[:, coff:coff + 256],
                                  wbuf[oc][:, dc * 128:(dc + 1) * 128],
                                  lnring[rg][:, dc, :],
                                  start=(dc == 0), stop=(dc == DC - 1),
                                  inc=(dc == DC - 1))
                    p.wait("scalar", "pe", qp)
                    q_act = p.sc("activation", Qv[:, oc, qc * 256:(qc + 1) * 256],
                                 psq[:, coff:coff + 256], AF.Identity,
                                 bias=bqc[:, hp * 2 + oc:hp * 2 + oc + 1])
                    qp_act4[pslot] = q_act
                lnld[rg] = p.pe
                if qc % 2 == 1:
                    i = qc // 2
                    if i == 0:
                        p.wait("vector", "pe", p.pe)
                        p.wait("vector", "act", p.act)
                    p.wait("vector", "act", q_act)
                    p.wait("vector", "pe", st["kp"][i % 2])
                    st["dv"][i] = p.ve(
                        "tensor_tensor",
                        (sqA, sqB)[i % 2].rearrange("p (s o) -> p s o", s=2),
                        Qv[:, :, i * 512:(i + 1) * 512],
                        Qv[:, :, i * 512:(i + 1) * 512], ALU.mult)
                    if i >= 1:
                        prep_mms(i - 1, st)
            prep_mms(15, st)
            wslot_free[0] = p.pe
            wslot_free[1] = p.pe
            if hp == 0:
                p.wait("sync", "cc", cc_ready)
                kddr = []
                vddr = []
                for r in range(8):
                    kddr.append(p.dma("sync", Ksb[:, r * 1024:(r + 1) * 1024],
                                      gout2_d[r * 2 * HD:r * 2 * HD + HD, :]))
                for r in range(8):
                    vddr.append(p.dma("sync", Vfm[:, r * 1024:(r + 1) * 1024],
                                      gout2_d[r * 2 * HD + HD:(r + 1) * 2 * HD, :]))
                p.wait_tok("tensor", vddr)
                vt_act = 0
                for g in range(8):
                    if vt_act:
                        p.wait("tensor", "act", vt_act)
                    vt = 0
                    for t in range(8):
                        vt = p.tr(psT[:, t * 128:(t + 1) * 128],
                                  Vfm[:, (g * 8 + t) * 128:(g * 8 + t + 1) * 128],
                                  id16[:, :], inc=(t == 7))
                    p.wait("scalar", "pe", vt)
                    for t in range(8):
                        vt_act = p.sc("copy",
                                      Vtm[:, (g * 8 + t) * 128:(g * 8 + t + 1) * 128],
                                      psT[:, t * 128:(t + 1) * 128], inc=(t == 7))
                p.wait_tok("scalar", kddr)
                p.wait("scalar", "pe", p.pe)
                p.wait("scalar", "dve", p.dve)
                ak_prev = 0
                for i in range(16):
                    aq = p.sc("activation", qsq16[:, :], Ksb[:, i * 512:(i + 1) * 512],
                              AF.Square, inc=True)
                    p.wait("tensor", "act", aq)
                    p.wait("tensor", "act", ak_prev)
                    kp = p.mm(psE[0:1, 0:512], ones_c[:, :], qsq16[:, :], inc=True)
                    p.wait("scalar", "pe", kp)
                    ak_prev = p.sc("copy", rows5f[0:1, :], psE[0:1, 0:512], inc=True)
                    p.wait("vector", "act", ak_prev)
                    p.ve("reduce_max", tinyf[0:1, 0:1], rows5f[0:1, :], axis=AX)
                    if i == 0:
                        km_dve = p.ve("tensor_copy", tinyf[0:1, 2:3], tinyf[0:1, 0:1])
                    else:
                        km_dve = p.ve("tensor_tensor", tinyf[0:1, 2:3],
                                      tinyf[0:1, 2:3], tinyf[0:1, 0:1], ALU.max)
            wo_pre = [None, None]
            for mc in range(2):
                wo_pre[mc], _ = load_w(mc, wo_d[mc], DC, 128)
            wo_pre = [None, None]
            for mc in range(2):
                wo_pre[mc], _ = load_w(mc, wo_d[mc], DC, 128)
            p.wait_tok("vector", p.fence_toks("sync"))
            p.wait("vector", "pe", p.pe)
            p.wait("vector", "dve", km_dve)
            p.wait("tensor", "act", att["psc_free"])
            anc = 0
            for hl2 in range(2):
                npr = p.ve("tensor_tensor", tinyf[0:1, 3:4],
                           tinyf[0:1, 6 + hl2:7 + hl2], tinyf[0:1, 2:3],
                           ALU.mult)
                p.wait("scalar", "dve", npr)
                p.sc("activation", tinyf[0:1, 4:5], tinyf[0:1, 3:4], AF.Ln)
                acx = p.sc("activation", tinyf[0:1, 5:6], tinyf[0:1, 4:5],
                           AF.Exp, scale=0.5, bias=cvec[0:1, 1:2],
                           selfwait=True)
                p.wait("vector", "act", acx)
                nc16 = p.ve("tensor_scalar", tiny16[0:1, 0:1],
                            tinyf[0:1, 5:6], -1.0, None, ALU.mult)
                p.wait("tensor", "dve", nc16)
                p.wait("tensor", "act", anc)
                ncp = p.mm(psC[:, 0:1], ones_r[:, :], tiny16[0:1, 0:1],
                           inc=True)
                p.wait("scalar", "pe", ncp)
                anc = p.sc("copy", negC2h[:, hl2:hl2 + 1], psC[:, 0:1])
            p.wait("scalar", "act", anc)
            p.wait("tensor", "act", p.act)
            for hl in range(2):
                psOb = psT.bitcast(F32)
                for qp in range(8):
                    qA = slice(qp * 1024, qp * 1024 + 512)
                    qB = slice(qp * 1024 + 512, qp * 1024 + 1024)
                    exp_cnt = [0] * 64
                    for kt in range(64):
                        bank = ps_ring[kt % 3]
                        p.wait("tensor", "act", att["last_exp"][kt % 3])
                        p.mm(bank[:, 0:512], Ksb[:, kt * 128:(kt + 1) * 128],
                             Qv[:, hl, qA], start=True, stop=True)
                        s_cnt = p.mm(bank[:, 512:1024],
                                     Ksb[:, kt * 128:(kt + 1) * 128],
                                     Qv[:, hl, qB], start=True, stop=True, inc=True)
                        p.wait("scalar", "pe", s_cnt)
                        p.wait("scalar", "pe", att["eslot_pe"][kt % 4])
                        p.wait("scalar", "dve", att["eslot_dve"][kt % 4])
                        exp_cnt[kt] = p.sc("activation", Ering[:, kt % 4, :],
                                           bank[:, :], AF.Exp, scale=S_ATTN,
                                           bias=negC2h[:, hl:hl + 1])
                        att["last_exp"][kt % 3] = exp_cnt[kt]
                        if kt >= 1:
                            _av2(kt - 1, exp_cnt[kt - 1], psOb)
                        if kt % 2 == 1:
                            kk = kt // 2
                            p.wait("vector", "act", exp_cnt[kt])
                            nm = p.ve("tensor_tensor", mrg16,
                                      Ering[:, (kt - 1) % 4, :],
                                      Ering[:, kt % 4, :], ALU.add)
                            att["eslot_dve"][(kt - 1) % 4] = nm
                            att["eslot_dve"][kt % 4] = nm
                            if kk == 0:
                                p.ve("tensor_copy", accP[0], mrg16)
                            else:
                                p.ve("tensor_tensor", accP[kk % 2],
                                     accP[(kk - 1) % 2], mrg16, ALU.add)
                    last_av = _av2(63, exp_cnt[63], psOb)
                    p.wait("scalar", "pe", last_av)
                    oca = p.sc("copy", O16[:, :], psC[:, :])
                    ocb = p.sc("copy", O16b[:, :], psOb)
                    p.wait("vector", "pe", last_av)
                    ncst = p.ve("tensor_copy", Ering[:, 0, :], accP[31 % 2])
                    att["eslot_dve"][0] = ncst
                    p.wait("tensor", "dve", ncst)
                    p.wait("tensor", "act", oca)
                    smm = 0
                    for t in range(4):
                        p.mm(psC[:, t:t + 1], Ering[:, 0, t * 128:(t + 1) * 128],
                             ones_c[:, :], start=True, stop=True)
                        smm = p.mm(psC[:, 4 + t:5 + t],
                                   Ering[:, 0, 512 + t * 128:512 + (t + 1) * 128],
                                   ones_c[:, :], start=True, stop=True,
                                   inc=(t == 3))
                    att["eslot_pe"][0] = smm
                    p.wait("scalar", "pe", smm)
                    nrc = 0
                    for i in range(8):
                        p.sc("activation", tinyf[:, 8 + i:9 + i], psC[:, i:i + 1],
                             AF.Ln)
                        nrc = p.sc("activation", tinyf[:, 8 + i:9 + i],
                                   tinyf[:, 8 + i:9 + i], AF.Exp, scale=-1.0,
                                   selfwait=True)
                    att["psc_free"] = nrc
                    p.wait("tensor", "act", ocb)
                    ot = 0
                    for t in range(4):
                        p.tr(psT[:, t * 128:(t + 1) * 128],
                             O16[:, t * 128:(t + 1) * 128], id16[:, :])
                    for t in range(4):
                        ot = p.tr(psT[:, 512 + t * 128:512 + (t + 1) * 128],
                                  O16b[:, t * 128:(t + 1) * 128], id16[:, :],
                                  inc=(t == 3))
                    p.wait("vector", "pe", ot)
                    p.wait("vector", "act", nrc)
                    nyt = 0
                    for half2 in range(2):
                        qc = qp * 2 + half2
                        par = qc // 8
                        for t in range(4):
                            cc = (qc % 8) * 4 + t
                            nyt = p.ve("tensor_scalar", Yt[:, cc, hl, :, par],
                                       psT[:, half2 * 512 + t * 128:
                                           half2 * 512 + (t + 1) * 128],
                                       tinyf[:, 8 + half2 * 4 + t:
                                             9 + half2 * 4 + t], None, ALU.mult)
                    att["psob_free"] = nyt
            if DEV_STOP == "p2" and hp == 0:
                p.wait("sync", "dve", p.dve)
                p.dma("sync", dbg_d, ybu[:, :])
            p.wait("tensor", "dve", p.dve)
            p.wait("sync", "dve", p.dve)
            wo_ps_dve = [0, 0]
            for mc in range(32):
                slot = mc % 2
                if mc < 2:
                    wo_dg = wo_pre[mc]
                    wov = wbuf[slot][:, 0:4096].rearrange("p (c w) -> p c w", w=128)
                else:
                    wo_dg, wov = load_w(slot, wo_d[mc], DC, 128)
                ps = psA if mc % 2 == 0 else psB
                p.wait_tok("tensor", wo_dg)
                p.wait("tensor", "dve", wo_ps_dve[mc % 2])
                idx = hp * 32 + mc
                page_load(idx, x1_d, mc, hp * 512)
                wp = 0
                for cc in range(DC):
                    wp = p.mm(ps[:, 0:512], wov[:, cc, :], YtC[:, cc, :],
                              start=(cc == 0), stop=(cc == DC - 1),
                              inc=(cc == DC - 1))
                p.wait("vector", "pe", wp)
                dst = x2_d if DEV_STOP != "p2" else y_d
                x2_last_store = page_evac_store(idx, ps[:, 0:512], dst, mc, hp * 512,
                                               bias_col=boc[:, mc:mc + 1])
                wo_ps_dve[mc % 2] = pg_dve[idx % 4]
                wslot_free[slot] = p.pe
                if hp == 1 and mc % 8 == 7:
                    x2_fen.append(p.fence_toks("sync"))
            att["last_exp"] = [0, 0, 0]
            att["eslot_pe"] = [0, 0, 0, 0]
            att["eslot_dve"] = [0, 0, 0, 0]
            att["psc_free"] = 0
            att["psob_free"] = 0
            p.wait("tensor", "act", p.act)
            p.wait("tensor", "dve", p.dve)
        return x2_last_store, x2_fen

    if DEV_STOP == "p1":
        plan_ffn(x_d, y_d, x_d, cached=False)
    else:
        st1, fen1, rbc1, mbc1, xbd1 = plan_ffn(x_d, x1_d, x_d, cached=False,
                                               fuse_ln1=True)
        cc1 = plan_ln1_gather(rbc1, mbc1, xbd1)
        st2, x2_fences = plan_attention(cc1)
        if DEV_STOP != "p2":
            p.join()
            plan_ffn(x2_d.rearrange("(c p) n -> p c n", p=128), y_d, x2_d,
                     cached=True, prev_fences=x2_fences)

    p.wait_tok("sync", p.fence_toks("sync"))
    p.wait_tok("gpsimd", p.fence_toks("gpsimd"))

    with nc.Block() as block:
        @block.tensor
        def _(E):
            for f in p.ops["tensor"]:
                f(E)

        @block.scalar
        def _(E):
            for f in p.ops["scalar"]:
                f(E)

        @block.vector
        def _(E):
            for f in p.ops["vector"]:
                f(E)

        @block.gpsimd
        def _(E):
            for f in p.ops["gpsimd"]:
                f(E)

        @block.sync
        def _(E):
            for f in p.ops["sync"]:
                f(E)

    es.close()
    return nc


_NC_CACHE = {}


def _get_nc():
    key = DEV_STOP
    if key not in _NC_CACHE:
        _NC_CACHE[key] = build_nc()
    return _NC_CACHE[key]


def kernel(L0_attn_hidden_state, L0_attn_Wq, L0_attn_bq, L0_attn_Wk, L0_attn_bk,
           L0_attn_Wv, L0_attn_bv, L0_attn_Wo, L0_attn_bo, L0_ffn_A, L0_ffn_B,
           L0_ffn_Wr):
    def pil(a, cb=128):
        R, C = a.shape
        return np.ascontiguousarray(
            a.reshape(R // 128, 128, C // cb, cb).transpose(2, 1, 0, 3))

    x = np.asarray(L0_attn_hidden_state, np.float32)[0].T
    Wq = np.asarray(L0_attn_Wq, np.float32)
    wr = pil(np.asarray(L0_ffn_Wr, np.float32).T, 64)[0]
    at = pil(np.asarray(L0_ffn_A, np.float32).reshape(1024, 4096).T)
    bt = pil(np.asarray(L0_ffn_B, np.float32).transpose(0, 2, 1).reshape(1024, 4096))
    wk = pil(np.asarray(L0_attn_Wk, np.float32).T)[0]
    wv = pil(np.asarray(L0_attn_Wv, np.float32).T)[0]
    wo = pil(np.asarray(L0_attn_Wo, np.float32).T)
    bq = np.asarray(L0_attn_bq, np.float32)
    bkc = np.ascontiguousarray(np.asarray(L0_attn_bk, np.float32).reshape(128, 1))
    bvc = np.ascontiguousarray(np.asarray(L0_attn_bv, np.float32).reshape(128, 1))
    boc = np.ascontiguousarray(np.asarray(L0_attn_bo, np.float32).reshape(32, 128).T)
    cvec = np.zeros((1, 8), np.float32)
    cvec[0, 0] = EPS
    cvec[0, 1] = float(np.log(S_ATTN))

    nc = _get_nc()
    in_maps = []
    for c in range(8):
        in_maps.append(dict(
            x=np.ascontiguousarray(
                x[:, c * TOK:(c + 1) * TOK].reshape(32, 128, TOK).transpose(1, 0, 2)),
            wr=wr, at=at, bt=bt,
            wq=pil(Wq[c * 512:(c + 1) * 512, :].T),
            wk=wk, wv=wv, wo=wo,
            bqc=np.ascontiguousarray(bq[c * 512:(c + 1) * 512].reshape(4, 128).T),
            bkc=bkc, bvc=bvc, boc=boc, cvec=cvec,
        ))
    res = run_bass_kernel_spmd(nc, in_maps, core_ids=list(range(8)))
    outs = [np.asarray(r["y"], np.float32).T for r in res.results]
    return np.concatenate(outs, axis=0)[None]
